# version 11
# baseline (speedup 1.0000x reference)
import numpy as np
import ml_dtypes
import concourse.bass as bass
import concourse.mybir as mybir
from concourse.bass_utils import run_bass_kernel_spmd

F32 = mybir.dt.float32
BF16 = mybir.dt.bfloat16
AX = mybir.AxisListType.X
ALU = mybir.AluOpType
ACTF = mybir.ActivationFunctionType

N_CORES = 8
D = 2048
NH = 16
TOK = 1024
NT = TOK // 128
H_FFN = 8192
EPS_LN = 1e-5
EPS_RMS = float(np.finfo(np.float32).eps)
AGW = D + 16

ENGS = ("tensor", "vector", "scalar", "sync", "gpsimd")

XFM = 0
QFM = 16384
WS1 = 32768
WS2 = 36864
WS3 = 40960
KVB = 45056
X1T = 45056
QU = 53248
KVG = 62976
NBF = 65280

XR = 0
ACC = 4096
Y2 = 12288
ELU = 14336
NF32 = 16896


class _Stop(Exception):
    pass


class Emitter:
    def __init__(self):
        self.q = {e: [] for e in ENGS}
        self.cnt = {}
        self.sems = {}

    def new_sem(self, name, stk, nc):
        self.sems[name] = stk.enter_context(nc.semaphore(name))
        self.cnt[name] = 0

    def emit(self, eng, fn):
        self.q[eng].append(fn)

    def wait(self, eng, sem, tok):
        if tok and tok > 0:
            s = self.sems[sem]
            self.emit(eng, lambda e, s=s, tok=tok: e.wait_ge(s, tok))

    def op(self, eng, fn, sem=None, n=1):
        tok = None
        if sem is not None:
            self.cnt[sem] += n
            tok = self.cnt[sem]
            s = self.sems[sem]

            def wrapped(e, fn=fn, s=s, n=n):
                fn(e).then_inc(s, n)
            self.emit(eng, wrapped)
        else:
            self.emit(eng, lambda e, fn=fn: fn(e))
        return tok

    def dma(self, eng, out, in_, sem, **kw):
        return self.op(
            eng, lambda e, out=out, in_=in_, kw=kw: e.dma_start(out=out, in_=in_, **kw),
            sem=sem, n=16)

    def tok(self, sem):
        return self.cnt[sem]


def build_nc(debug=False, stage=99):
    nc = bass.Bass("TRN2", target_bir_lowering=False, detect_race_conditions=False)

    x_h = nc.dram_tensor("x", [TOK, D], F32, kind="ExternalInput").ap()
    wT_h = {n: nc.dram_tensor(n, [D, D], F32, kind="ExternalInput").ap()
            for n in ("wqT", "wkT", "wvT", "woT")}
    b_h = {n: nc.dram_tensor(n, [1, D], F32, kind="ExternalInput").ap()
           for n in ("bq", "bk", "bv", "bo")}
    wguT_h = nc.dram_tensor("wguT", [D, D], F32, kind="ExternalInput").ap()
    wprojT_h = nc.dram_tensor("wprojT", [1024, D], F32, kind="ExternalInput").ap()
    ident_h = nc.dram_tensor("ident", [128, 128], F32, kind="ExternalInput").ap()
    identb_h = nc.dram_tensor("identb", [128, 128], BF16, kind="ExternalInput").ap()
    consts_h = nc.dram_tensor("consts_bf", [128, 4], BF16, kind="ExternalInput").ap()
    sel2_h = nc.dram_tensor("sel2", [2, 128], BF16, kind="ExternalInput").ap()
    sel2f_h = nc.dram_tensor("sel2f", [2, 128], F32, kind="ExternalInput").ap()
    rowone_h = nc.dram_tensor("rowone", [1, 128], BF16, kind="ExternalInput").ap()
    hsel_h = nc.dram_tensor("hsel", [16, 16 * 128], F32, kind="ExternalInput").ap()
    bmask_h = nc.dram_tensor("bmask", [128, 2], F32, kind="ExternalInput").ap()
    y_h = nc.dram_tensor("y", [TOK, D], F32, kind="ExternalOutput").ap()

    qc_in = nc.dram_tensor("qc_in", [3072, D], BF16)
    qc_out = nc.dram_tensor("qc_out", [N_CORES * 3072, D], BF16)
    ag2_in = nc.dram_tensor("ag2_in", [128, AGW], F32)
    ag2_out = nc.dram_tensor("ag2_out", [N_CORES * 128, AGW], F32)

    dbg = {}
    if debug:
        for name, shape, dt in [
            ("x_fm", [128, 16 * TOK], BF16), ("q_fm", [128, 16 * TOK], BF16),
            ("kvp", [128, AGW], F32), ("kvg", [128, AGW], BF16),
            ("what_gu0", [128, D], BF16), ("what_pj0", [128, D], BF16),
            ("rq", [128, 8 * 16], F32), ("qksraw", [128, 8 * 16], F32), ("y1", [128, D], F32), ("st1", [128, 16], F32),
            ("attn", [128, 16 * TOK], BF16), ("x1", [128, 8 * D], BF16),
            ("h0", [128, 512], BF16), ("acc0", [128, D], F32),
            ("rqr", [1, TOK], F32),
        ]:
            dbg[name] = nc.dram_tensor("dbg_" + name, shape, dt, kind="ExternalOutput").ap()

    from contextlib import ExitStack
    with ExitStack() as stk:
        def sbt(name, shape, dt):
            return stk.enter_context(nc.sbuf_tensor(name, shape, dt))

        AB = sbt("AB", [128, NBF], BF16)
        AF = sbt("AF", [128, NF32], F32)
        ident_s = sbt("ident_s", [128, 128], F32)
        identb_s = sbt("identb_s", [128, 128], BF16)
        consts = sbt("consts_s", [128, 4], BF16)
        sel2 = sbt("sel2_s", [2, 128], BF16)
        sel2f = sbt("sel2f_s", [2, 128], F32)
        rowone = sbt("rowone_s", [1, 128], BF16)
        bmask = sbt("bmask_s", [128, 2], F32)
        scT = sbt("scT", [2, D], BF16)
        stats = sbt("stats", [128, 16], F32)
        rq_tm = sbt("rq_tm", [128, 8 * 16], F32)
        qksraw = sbt("qksraw", [128, 8 * 16], F32)
        rqr_tm = sbt("rqr_tm", [128, 8], F32)
        bq_fm = sbt("bq_fm", [128, 16], F32)
        nbq_fm = sbt("nbq_fm", [128, 16], F32)
        bk_fm = sbt("bk_fm", [128, 16], F32)
        nbk_fm = sbt("nbk_fm", [128, 16], F32)
        bv_fm = sbt("bv_fm", [128, 16], F32)

        psA = stk.enter_context(nc.psum_tensor("psA", [128, 2048], F32))
        psB = stk.enter_context(nc.psum_tensor("psB", [128, 2048], F32))

        rq_fm = AF[0:16, ACC + 2064: ACC + 2064 + TOK]
        recT = AF[0:2, Y2: Y2 + D]
        hsel = AF[0:16, ACC + 3088: ACC + 3088 + 16 * 128]
        h2row = AF[0:1, ELU + 1536: ELU + 2048]
        rqr_row = AF[0:1, ELU + 1024: ELU + 1536]
        borow = AB[0:1, WS2: WS2 + D]

        E = Emitter()
        for s in ("gd", "sd", "mi", "x0", "x1", "qw0", "qw1", "st0", "st1",
                  "wq0", "wq1", "wk0", "wk1", "wv0", "wv1", "wo0", "wo1",
                  "gg0", "gg1", "gu0", "gu1", "wp", "xf", "ag", "cc1", "cc2",
                  "ra", "rb", "rc", "rd",
                  "te", "ve", "se", "ge", "od"):
            E.new_sem(s, stk, nc)

        ones_col = consts[:, 0:1]
        gsel = consts[:, 1:3]

        def ckpt(k):
            if stage < k:
                raise _Stop()

        try:
            for out, in_ in [(ident_s[:], ident_h[:, :]), (identb_s[:], identb_h[:, :]),
                             (consts[:], consts_h[:, :]), (sel2[:], sel2_h[:, :]),
                             (rowone[:], rowone_h[:, :]), (sel2f[:], sel2f_h[:, :]),
                             (bmask[:], bmask_h[:, :])]:
                E.dma("sync", out, in_, "mi")
            E.dma("sync", bq_fm[:], b_h["bq"][0:1, :].rearrange("o (c p) -> (o p) c", p=128), "mi", allow_slow_non_contiguous=True)
            E.dma("sync", bk_fm[:], b_h["bk"][0:1, :].rearrange("o (c p) -> (o p) c", p=128), "mi", allow_slow_non_contiguous=True)
            E.dma("sync", bv_fm[:], b_h["bv"][0:1, :].rearrange("o (c p) -> (o p) c", p=128), "mi", allow_slow_non_contiguous=True)
            tok_bias = E.tok("mi")
            E.wait("vector", "mi", tok_bias)
            E.op("vector", lambda v: v.tensor_scalar(nbq_fm[:], bq_fm[:], -1.0, None, ALU.mult), "ve")
            E.op("vector", lambda v: v.tensor_scalar(nbk_fm[:], bk_fm[:], -1.0, None, ALU.mult), "ve")
            tok_nb = E.tok("ve")

            x_tiled = x_h.rearrange("(t p) d -> t p d", p=128)

            def xroll(i):
                return AF[:, XR + (i % 2) * D: XR + (i % 2) * D + D]

            xtok = [0] * NT
            xtok[0] = E.dma("sync", xroll(0), x_tiled[0], "x0")
            xtok[1] = E.dma("sync", xroll(1), x_tiled[1], "x1")

            E.wait("tensor", "mi", tok_bias)
            tp_evac = [0, 0, 0, 0]
            for t in range(NT):
                E.wait("tensor", "x%d" % (t % 2), xtok[t])
                for dc in range(16):
                    r = dc % 4
                    E.wait("tensor", "ve", tp_evac[r])
                    mt = E.op("tensor", lambda te, r=r, t=t, dc=dc: te.transpose(
                        psA[:, r * 512:r * 512 + 128],
                        xroll(t)[:, dc * 128:dc * 128 + 128], ident_s[:]), "te")
                    E.wait("vector", "te", mt)
                    tp_evac[r] = E.op("vector", lambda v, r=r, t=t, dc=dc: v.tensor_copy(
                        AB[:, XFM + dc * TOK + t * 128: XFM + dc * TOK + t * 128 + 128],
                        psA[:, r * 512:r * 512 + 128]), "ve")
                if t + 2 < NT:
                    E.wait("sync", "te", E.tok("te"))
                    xtok[t + 2] = E.dma("sync", xroll(t + 2), x_tiled[t + 2], "x%d" % (t % 2))
            tok_xfm_ve = E.tok("ve")
            if debug:
                E.wait("sync", "ve", tok_xfm_ve)
                E.dma("sync", dbg["x_fm"], AB[:, XFM:XFM + 16 * TOK], "od")

            ckpt(2)
            def qw(hh):
                return AB[:, QU + hh * 1024: QU + (hh + 1) * 1024]
            aw = AB[:, QU + 2048: QU + 3072]

            def ssb(r):
                return AB[:, QU + 3072 + r * 512: QU + 3072 + (r + 1) * 512]
            rt = AB[:, QU + 4096: QU + 4608]
            qn_t = AB[:, QU + 4608: QU + 5120]
            qq_t = AB[:, QU + 5120: QU + 5632]

            def what(g):
                return AB[:, QU + 5632 + g * 2048: QU + 5632 + (g + 1) * 2048]

            def srec(r):
                return AF[:, ACC + r * 512: ACC + (r + 1) * 512]

            _qs = {"bc_ve": [0, 0], "bc_se": [0, 0], "wh": [0], "sc": [0],
                   "st": [0, 0]}

            def emit_quant(src_h, n_chunks, out_row0, dbg_key=None):
                E.wait("gpsimd", "ve", E.tok("ve"))
                qload = {(0, 0): E.dma("gpsimd", qw(0), src_h[0:128, 0:1024], "qw0")}
                bc_free_ve = _qs["bc_ve"]
                bc_free_se = _qs["bc_se"]
                wh_tok = _qs["wh"]
                st_free = _qs["st"]
                for c in range(n_chunks):
                    for hh in range(2):
                        ld = qload.pop((c, hh))
                        E.wait("scalar", "qw%d" % hh, ld)
                        E.wait("scalar", "te", E.tok("te"))
                        E.op("scalar", lambda s, hh=hh: s.activation(
                            aw[:], qw(hh), ACTF.Abs), "se")
                        awtok = E.tok("se")
                        sctoks = []
                        for sl in range(2):
                            E.wait("tensor", "se", awtok)
                            E.wait("tensor", "ve", _qs["sc"][0])
                            mt = E.op("tensor", lambda te, sl=sl: te.matmul(
                                psB[0:2, 1536:2048], gsel, aw[:, sl * 512:(sl + 1) * 512],
                                start=True, stop=True), "te")
                            E.wait("vector", "te", mt)
                            _qs["sc"][0] = E.op("vector", lambda v, hh=hh, sl=sl: v.tensor_scalar(
                                scT[0:2, hh * 1024 + sl * 512: hh * 1024 + (sl + 1) * 512],
                                psB[0:2, 1536:2048], 1.0 / 64.0, 1e-8, ALU.mult, ALU.max), "ve")
                        sctok = _qs["sc"][0]
                        E.wait("vector", "ve", sctok)
                        E.wait("vector", "ve", E.tok("ve"))
                        E.op("vector", lambda v, hh=hh: v.reciprocal(
                            recT[0:2, hh * 1024:(hh + 1) * 1024],
                            scT[0:2, hh * 1024:(hh + 1) * 1024]), "ve")
                        rectok = E.tok("ve")
                        nxt = (c, 1) if hh == 0 else (c + 1, 0)
                        if nxt[0] < n_chunks:
                            E.wait("gpsimd", "ve", E.tok("ve"))
                            qload[nxt] = E.dma("gpsimd", qw(nxt[1]),
                                               src_h[nxt[0] * 128:(nxt[0] + 1) * 128,
                                                     nxt[1] * 1024:(nxt[1] + 1) * 1024],
                                               "qw%d" % nxt[1])
                        for sl in range(2):
                            col = hh * 1024 + sl * 512
                            E.wait("tensor", "se", bc_free_se[0])
                            E.wait("tensor", "ve", sctok)
                            bt = E.op("tensor", lambda te, col=col: te.matmul(
                                psB[:, 0:512], sel2[:],
                                scT[0:2, col:col + 512], start=True, stop=True), "te")
                            E.wait("tensor", "ve", bc_free_ve[1])
                            E.wait("tensor", "ve", rectok)
                            bt2 = E.op("tensor", lambda te, col=col: te.matmul(
                                psB[:, 512:1024], sel2f[:],
                                recT[0:2, col:col + 512], start=True, stop=True), "te")
                            E.wait("scalar", "te", bt)
                            E.wait("scalar", "ve", wh_tok[0])
                            bc_free_se[0] = E.op("scalar", lambda s: s.copy(
                                ssb(0), psB[:, 0:512]), "se")
                            E.wait("vector", "te", bt2)
                            E.op("vector", lambda v, hh=hh, sl=sl: v.tensor_tensor(
                                rt[:], qw(hh)[:, sl * 512:(sl + 1) * 512],
                                psB[:, 512:1024], ALU.mult), "ve")
                            bc_free_ve[1] = E.tok("ve")
                            E.op("vector", lambda v: v.tensor_scalar(
                                qn_t[:], rt[:], -0.5, None, ALU.is_lt), "ve")
                            E.op("vector", lambda v: v.scalar_tensor_tensor(
                                qq_t[:], rt[:], 0.5, qn_t[:], ALU.is_gt, ALU.subtract), "ve")
                            stg = c % 2
                            E.wait("vector", "se", bc_free_se[0])
                            E.wait("vector", "st%d" % stg, st_free[stg])
                            E.op("vector", lambda v, stg=stg, col=col: v.tensor_tensor(
                                what(stg)[:, col:col + 512], ssb(0), qq_t[:], ALU.mult), "ve")
                            wh_tok[0] = E.tok("ve")
                    stg = c % 2
                    wtok = E.tok("ve")
                    E.wait("sync", "ve", wtok)
                    st_free[stg] = E.dma(
                        "sync",
                        qc_in.ap()[out_row0 + c * 128: out_row0 + (c + 1) * 128, :],
                        what(stg), "st%d" % stg)
                    if debug and dbg_key and c == 0:
                        E.dma("sync", dbg[dbg_key], what(stg), "od")
                return None

            emit_quant(wguT_h, 16, 0, "what_gu0")
            emit_quant(wprojT_h, 8, D, "what_pj0")
            tok_st0, tok_st1 = E.tok("st0"), E.tok("st1")
            tok_q_ve = E.tok("ve")
            tok_q_se = E.tok("se")

            ckpt(2.2)
            E.wait("gpsimd", "st0", tok_st0)
            E.wait("gpsimd", "st1", tok_st1)
            E.op("gpsimd", lambda g: g.collective_compute(
                "AllGather", ALU.bypass, replica_groups=[list(range(N_CORES))],
                ins=[qc_in.ap().opt()],
                outs=[qc_out.ap().opt()]), "cc1")
            tok_ag1 = E.tok("cc1")

            ckpt(3)
            elu_t = AF[:, ELU: ELU + 512]
            elu_e = AF[:, ELU + 512: ELU + 1024]
            elu_s = AF[:, ELU + 1024: ELU + 1536]

            def load_wcol(w_ap, fc, buf, base, sem):
                return E.dma(
                    "gpsimd",
                    AB[:, base + buf * D: base + (buf + 1) * D].rearrange("p (dc j) -> p dc j", j=128),
                    w_ap[:, fc * 128:(fc + 1) * 128].rearrange("(dc p) j -> p dc j", p=128), sem)

            elu_guard = [0]

            def emit_elu_epilogue(psum_ap, bias_col, nbias_col, dst_ap):
                E.wait("scalar", "ve", elu_guard[0])
                E.op("scalar", lambda s: s.activation(
                    elu_t[:], psum_ap, ACTF.Relu, bias=nbias_col, scale=-1.0), "se")
                ttok = E.tok("se")
                E.op("scalar", lambda s: s.activation(
                    elu_e[:], elu_t[:], ACTF.Exp, scale=-1.0), "se")
                etok = E.tok("se")
                E.wait("vector", "se", ttok)
                E.op("vector", lambda v: v.scalar_tensor_tensor(
                    elu_s[:], psum_ap, bias_col, elu_t[:], ALU.add, ALU.add), "ve")
                ve1 = E.tok("ve")
                E.wait("vector", "se", etok)
                E.op("vector", lambda v: v.tensor_tensor(
                    dst_ap, elu_e[:], elu_s[:], ALU.add), "ve")
                elu_guard[0] = E.tok("ve")
                return ve1, elu_guard[0]

            wqtok = [0, 0]
            wqtok[0] = load_wcol(wT_h["wqT"], 0, 0, WS1, "wq0")
            wqtok[1] = load_wcol(wT_h["wqT"], 1, 1, WS1, "wq1")
            q_evac_ve = [0, 0]
            q_evac_se = [0, 0]
            E.wait("tensor", "ve", tok_xfm_ve)
            for fc in range(16):
                buf = fc % 2
                E.wait("tensor", "wq%d" % buf, wqtok[buf])
                for half in range(2):
                    r = half
                    E.wait("tensor", "ve", q_evac_ve[r])
                    E.wait("tensor", "se", q_evac_se[r])
                    for dc in range(16):
                        E.op("tensor", lambda te, r=r, buf=buf, dc=dc, half=half: te.matmul(
                            psA[:, r * 512:(r + 1) * 512],
                            AB[:, WS1 + buf * D + dc * 128: WS1 + buf * D + dc * 128 + 128],
                            AB[:, XFM + dc * TOK + half * 512: XFM + dc * TOK + half * 512 + 512],
                            start=(dc == 0), stop=(dc == 15)), "te")
                    mtok = E.tok("te")
                    E.wait("scalar", "te", mtok)
                    E.wait("scalar", "ve", tok_nb)
                    ve1, ve2 = emit_elu_epilogue(
                        psA[:, r * 512:(r + 1) * 512], bq_fm[:, fc:fc + 1], nbq_fm[:, fc:fc + 1],
                        AB[:, QFM + fc * TOK + half * 512: QFM + fc * TOK + half * 512 + 512])
                    q_evac_ve[r] = ve1
                    q_evac_se[r] = E.tok("se")
                if fc + 2 < 16:
                    E.wait("gpsimd", "te", E.tok("te"))
                    wqtok[buf] = load_wcol(wT_h["wqT"], fc + 2, buf, WS1, "wq%d" % buf)
            tok_qdone = E.tok("ve")
            if debug:
                E.wait("sync", "ve", tok_qdone)
                E.dma("sync", dbg["q_fm"], AB[:, QFM:QFM + 16 * TOK], "od")

            ckpt(4)
            def kfm(b):
                return AF[:, ACC + 4096 + b * 1024: ACC + 4096 + (b + 1) * 1024]

            def vfm(b):
                return AF[:, ACC + 6144 + b * 1024: ACC + 6144 + (b + 1) * 1024]

            def ktm(b):
                return AB[:, KVB + b * 1024: KVB + (b + 1) * 1024]

            def vtm(b):
                return AB[:, KVB + 2048 + b * 1024: KVB + 2048 + (b + 1) * 1024]

            wktok = [0, 0]
            wvtok = [0, 0]
            wktok[0] = load_wcol(wT_h["wkT"], 0, 0, WS2, "wk0")
            wvtok[0] = load_wcol(wT_h["wvT"], 0, 0, WS3, "wv0")
            wktok[1] = load_wcol(wT_h["wkT"], 1, 1, WS2, "wk1")
            wvtok[1] = load_wcol(wT_h["wvT"], 1, 1, WS3, "wv1")

            E.wait("tensor", "ve", tok_q_ve)
            E.wait("tensor", "se", tok_q_se)

            kv_evac = [0, 0]
            kv_se = [0, 0]
            tp_free = [0, 0]
            kvslot_free = [0, 0]
            buf_free = [0, 0]
            TPRE = [0, 512]
            SLOT = [1024, 1536]

            for h in range(NH):
                bb = h % 2
                E.wait("vector", "te", buf_free[bb])
                E.wait("scalar", "te", buf_free[bb])
                E.wait("tensor", "wk%d" % bb, wktok[bb])
                for half in range(2):
                    r = 2 + half
                    E.wait("tensor", "ve", kv_evac[half])
                    E.wait("tensor", "se", kv_se[half])
                    for dc in range(16):
                        E.op("tensor", lambda te, r=r, bb=bb, dc=dc, half=half: te.matmul(
                            psA[:, r * 512:(r + 1) * 512],
                            AB[:, WS2 + bb * D + dc * 128: WS2 + bb * D + dc * 128 + 128],
                            AB[:, XFM + dc * TOK + half * 512: XFM + dc * TOK + half * 512 + 512],
                            start=(dc == 0), stop=(dc == 15)), "te")
                    mtok = E.tok("te")
                    E.wait("scalar", "te", mtok)
                    ve1, _ = emit_elu_epilogue(
                        psA[:, r * 512:(r + 1) * 512], bk_fm[:, h:h + 1], nbk_fm[:, h:h + 1],
                        kfm(bb)[:, half * 512:(half + 1) * 512])
                    kv_evac[half] = ve1
                    kv_se[half] = E.tok("se")
                kfm_tok = E.tok("ve")
                E.op("vector", lambda v, h=h, bb=bb: v.reduce_sum(
                    AF[:, ACC + 2048 + h: ACC + 2048 + h + 1], kfm(bb)[:, :], axis=AX), "ve")
                for t in range(NT):
                    rr = t % 2
                    E.wait("tensor", "ve", kfm_tok)
                    E.wait("tensor", "se", tp_free[rr])
                    mt = E.op("tensor", lambda te, rr=rr, bb=bb, t=t: te.transpose(
                        psB[:, TPRE[rr]:TPRE[rr] + 128],
                        kfm(bb)[:, t * 128:(t + 1) * 128], ident_s[:]), "te")
                    E.wait("scalar", "te", mt)
                    tp_free[rr] = E.op("scalar", lambda s, rr=rr, bb=bb, t=t: s.copy(
                        ktm(bb)[:, t * 128:(t + 1) * 128], psB[:, TPRE[rr]:TPRE[rr] + 128]), "se")
                ktm_tok = E.tok("se")
                E.wait("tensor", "wv%d" % bb, wvtok[bb])
                for half in range(2):
                    r = 2 + half
                    E.wait("tensor", "ve", kv_evac[half])
                    E.wait("tensor", "se", kv_se[half])
                    for dc in range(16):
                        E.op("tensor", lambda te, r=r, bb=bb, dc=dc, half=half: te.matmul(
                            psA[:, r * 512:(r + 1) * 512],
                            AB[:, WS3 + bb * D + dc * 128: WS3 + bb * D + dc * 128 + 128],
                            AB[:, XFM + dc * TOK + half * 512: XFM + dc * TOK + half * 512 + 512],
                            start=(dc == 0), stop=(dc == 15)), "te")
                    mtok = E.tok("te")
                    E.wait("vector", "te", mtok)
                    E.op("vector", lambda v, r=r, h=h, bb=bb, half=half: v.tensor_scalar(
                        vfm(bb)[:, half * 512:(half + 1) * 512],
                        psA[:, r * 512:(r + 1) * 512], bv_fm[:, h:h + 1], None, ALU.add), "ve")
                    kv_evac[half] = E.tok("ve")
                vfm_tok = E.tok("ve")
                for t in range(NT):
                    rr = t % 2
                    E.wait("tensor", "ve", vfm_tok)
                    E.wait("tensor", "se", tp_free[rr])
                    mt = E.op("tensor", lambda te, rr=rr, bb=bb, t=t: te.transpose(
                        psB[:, TPRE[rr]:TPRE[rr] + 128],
                        vfm(bb)[:, t * 128:(t + 1) * 128], ident_s[:]), "te")
                    E.wait("scalar", "te", mt)
                    tp_free[rr] = E.op("scalar", lambda s, rr=rr, bb=bb, t=t: s.copy(
                        vtm(bb)[:, t * 128:(t + 1) * 128], psB[:, TPRE[rr]:TPRE[rr] + 128]), "se")
                vtm_tok = E.tok("se")
                slot = h % 2
                E.wait("tensor", "se", ktm_tok)
                E.wait("tensor", "se", vtm_tok)
                E.wait("tensor", "ve", kvslot_free[slot])
                for t in range(NT):
                    E.op("tensor", lambda te, slot=slot, bb=bb, t=t: te.matmul(
                        psB[:, SLOT[slot]:SLOT[slot] + 128],
                        ktm(bb)[:, t * 128:(t + 1) * 128],
                        vtm(bb)[:, t * 128:(t + 1) * 128],
                        start=(t == 0), stop=(t == NT - 1)), "te")
                kv_mm = E.tok("te")
                buf_free[bb] = kv_mm
                E.wait("vector", "te", kv_mm)
                kvslot_free[slot] = E.op("vector", lambda v, slot=slot, h=h: v.tensor_copy(
                    AF[:, ACC + h * 128: ACC + (h + 1) * 128],
                    psB[:, SLOT[slot]:SLOT[slot] + 128]), "ve")
                if h + 2 < NH:
                    E.wait("gpsimd", "te", kv_mm)
                    wktok[bb] = load_wcol(wT_h["wkT"], h + 2, bb, WS2, "wk%d" % bb)
                    wvtok[bb] = load_wcol(wT_h["wvT"], h + 2, bb, WS3, "wv%d" % bb)

            tok_kv_all = E.tok("ve")
            E.wait("sync", "ve", tok_kv_all)
            tok_ag2in = E.dma("sync", ag2_in.ap()[:, :], AF[:, ACC:ACC + AGW], "ag")
            if debug:
                E.dma("sync", dbg["kvp"], AF[:, ACC:ACC + AGW], "od")
            E.wait("gpsimd", "ag", tok_ag2in)
            E.op("gpsimd", lambda g: g.collective_compute(
                "AllGather", ALU.bypass, replica_groups=[list(range(N_CORES))],
                ins=[ag2_in.ap().opt()], outs=[ag2_out.ap().opt()]), "cc2")
            tok_ag2 = E.tok("cc2")

            ckpt(5)
            kvb0 = AF[:, ACC + 2064: ACC + 2064 + AGW]
            kvb1 = AF[:, ACC + 4128: ACC + 4128 + AGW]
            tmpa = AF[:, ACC: ACC + AGW]
            tmpb = AF[:, ACC + 6192: ACC + 6192 + AGW]
            agb = ag2_out.ap()
            E.wait("gpsimd", "cc2", tok_ag2)
            t0 = E.dma("gpsimd", kvb0, agb[0 * 128:1 * 128, :], "ra")
            t1 = E.dma("gpsimd", kvb1, agb[4 * 128:5 * 128, :], "rb")
            tadd0 = tadd1 = 0
            for j in range(3):
                E.wait("gpsimd", "ve", tadd0)
                ta = E.dma("gpsimd", tmpa, agb[(1 + j) * 128:(2 + j) * 128, :], "rc")
                E.wait("gpsimd", "ve", tadd1)
                tb = E.dma("gpsimd", tmpb, agb[(5 + j) * 128:(6 + j) * 128, :], "rd")
                if j == 0:
                    E.wait("vector", "ra", t0)
                    E.wait("vector", "rb", t1)
                E.wait("vector", "rc", ta)
                E.op("vector", lambda v: v.tensor_tensor(kvb0, kvb0, tmpa, ALU.add), "ve")
                tadd0 = E.tok("ve")
                E.wait("vector", "rd", tb)
                E.op("vector", lambda v: v.tensor_tensor(kvb1, kvb1, tmpb, ALU.add), "ve")
                tadd1 = E.tok("ve")
            E.op("vector", lambda v: v.tensor_scalar(
                tmpb, kvb0, bmask[:, 0:1], None, ALU.mult), "ve")
            E.op("vector", lambda v: v.scalar_tensor_tensor(
                AB[:, KVG:KVG + AGW], kvb1, bmask[:, 1:2], tmpb, ALU.mult, ALU.add), "ve")
            tok_kvg = E.tok("ve")
            kvg = AB[:, KVG:KVG + AGW]
            E.wait("sync", "ve", tok_kvg)
            tok_hsel = E.dma("sync", hsel, hsel_h[:, :], "mi")
            E.wait("gpsimd", "ve", tok_q_ve)
            tok_borow = E.dma("gpsimd", borow, b_h["bo"][0:1, :], "gd")
            if debug:
                E.wait("sync", "ve", tok_kvg)
                E.dma("sync", dbg["kvg"], kvg, "od")

            ckpt(6)
            def qks_ps(t):
                return psB[:, 1024 + (t % 2) * 512: 1024 + (t % 2) * 512 + 16]

            qks_free = [0, 0]
            rqT_free = [0, 0, 0, 0]
            E.wait("tensor", "ve", tok_kvg)
            E.wait("tensor", "se", E.tok("se"))
            for t in range(NT):
                E.wait("tensor", "ve", qks_free[t % 2])
                for h in range(NH):
                    E.op("tensor", lambda te, t=t, h=h: te.matmul(
                        qks_ps(t)[:, h:h + 1],
                        AB[:, QFM + h * TOK + t * 128: QFM + h * TOK + t * 128 + 128],
                        kvg[:, 2048 + h:2048 + h + 1],
                        start=True, stop=True), "te")
                mtok = E.tok("te")
                E.wait("vector", "te", mtok)
                E.op("vector", lambda v, t=t: v.tensor_scalar(
                    qksraw[:, t * 16:(t + 1) * 16], qks_ps(t), 1e-8, None, ALU.add), "ve")
                qks_free[t % 2] = E.tok("ve")
                E.wait("scalar", "ve", qks_free[t % 2])
                E.op("scalar", lambda s, t=t: s.activation(
                    stats[:, 0:16], qksraw[:, t * 16:(t + 1) * 16], ACTF.Ln), "se")
                E.op("scalar", lambda s, t=t: s.activation(
                    rq_tm[:, t * 16:(t + 1) * 16], stats[:, 0:16], ACTF.Exp, scale=-1.0), "se")
                rcp_tok = E.tok("se")
                rr = t % 4
                E.wait("tensor", "se", rcp_tok)
                E.wait("tensor", "ve", rqT_free[rr])
                mt = E.op("tensor", lambda te, rr=rr, t=t: te.transpose(
                    psA[0:16, rr * 512:rr * 512 + 128],
                    rq_tm[:, t * 16:(t + 1) * 16], ident_s[:]), "te")
                E.wait("vector", "te", mt)
                rqT_free[rr] = E.op("vector", lambda v, rr=rr, t=t: v.tensor_copy(
                    rq_fm[:, t * 128:(t + 1) * 128], psA[0:16, rr * 512:rr * 512 + 128]), "ve")
            tok_rq = E.tok("ve")
            if debug:
                E.wait("sync", "ve", tok_rq)
                E.dma("sync", dbg["rq"], rq_tm[:, :], "od")
                E.dma("sync", dbg["qksraw"], qksraw[:, :], "od")

            qkv_free = [0, 0]
            rqb_free = [0, 0]

            def rqb_sb(r):
                return AF[:, ELU + 1024 + r * 512: ELU + 1024 + (r + 1) * 512]

            E.wait("tensor", "ve", tok_rq)
            E.wait("tensor", "mi", tok_hsel)
            for h in range(NH):
                for half in range(2):
                    r = half
                    E.wait("tensor", "ve", qkv_free[r])
                    E.op("tensor", lambda te, h=h, half=half, r=r: te.matmul(
                        psA[:, r * 512:(r + 1) * 512],
                        kvg[:, h * 128:(h + 1) * 128],
                        AB[:, QFM + h * TOK + half * 512: QFM + h * TOK + half * 512 + 512],
                        start=True, stop=True), "te")
                    qkv_tok = E.tok("te")
                    E.wait("tensor", "se", rqb_free[r])
                    E.op("tensor", lambda te, h=h, half=half, r=r: te.matmul(
                        psB[:, r * 512:(r + 1) * 512],
                        hsel[:, h * 128:(h + 1) * 128],
                        rq_fm[:, half * 512:(half + 1) * 512],
                        start=True, stop=True), "te")
                    rqb_tok = E.tok("te")
                    E.wait("scalar", "te", rqb_tok)
                    rqb_free[r] = E.op("scalar", lambda s, r=r: s.copy(
                        rqb_sb(r), psB[:, r * 512:(r + 1) * 512]), "se")
                    E.wait("vector", "te", qkv_tok)
                    E.wait("vector", "se", rqb_free[r])
                    qkv_free[r] = E.op("vector", lambda v, h=h, half=half, r=r: v.tensor_tensor(
                        AB[:, XFM + h * TOK + half * 512: XFM + h * TOK + half * 512 + 512],
                        psA[:, r * 512:(r + 1) * 512], rqb_sb(r), ALU.mult), "ve")
            tok_attn = E.tok("ve")
            if debug:
                E.wait("sync", "ve", tok_attn)
                E.dma("sync", dbg["attn"], AB[:, XFM:XFM + 16 * TOK], "od")

            ckpt(7)
            def load_wo(dc, buf):
                return E.dma("gpsimd", AB[:, WS1 + buf * D: WS1 + (buf + 1) * D],
                             wT_h["woT"][dc * 128:(dc + 1) * 128, :], "wo%d" % buf)

            wotok = [load_wo(0, 0), load_wo(1, 1)]
            ro_free = [0, 0]
            xrtok = [0] * NT
            E.wait("tensor", "ve", tok_attn)
            E.wait("tensor", "gd", E.tok("gd"))
            xrtok[0] = E.dma("sync", xroll(0), x_tiled[0], "x0")
            xrtok[1] = E.dma("sync", xroll(1), x_tiled[1], "x1")
            x1_done = [0] * NT
            for p in range(4):
                E.wait("tensor", "ve", ro_free[0])
                E.wait("tensor", "ve", ro_free[1])
                for dc in range(16):
                    buf = dc % 2
                    E.wait("tensor", "wo%d" % buf, wotok[buf])
                    for ti, ps in ((2 * p, psA), (2 * p + 1, psB)):
                        for sl in range(4):
                            E.op("tensor", lambda te, ti=ti, ps=ps, dc=dc, buf=buf, sl=sl: te.matmul(
                                ps[:, sl * 512:(sl + 1) * 512],
                                AB[:, XFM + dc * TOK + ti * 128: XFM + dc * TOK + ti * 128 + 128],
                                AB[:, WS1 + buf * D + sl * 512: WS1 + buf * D + sl * 512 + 512],
                                start=(dc == 0), stop=False), "te")
                    if dc + 2 < 16:
                        E.wait("gpsimd", "te", E.tok("te"))
                        wotok[buf] = load_wo(dc + 2, buf)
                    elif p < 3:
                        E.wait("gpsimd", "te", E.tok("te"))
                        wotok[buf] = load_wo(dc + 2 - 16, buf)
                for ti, ps in ((2 * p, psA), (2 * p + 1, psB)):
                    for sl in range(4):
                        E.op("tensor", lambda te, ps=ps, sl=sl: te.matmul(
                            ps[:, sl * 512:(sl + 1) * 512], rowone[:],
                            borow[0:1, sl * 512:(sl + 1) * 512], start=False, stop=True), "te")
                ro_tok = E.tok("te")
                for ti, ps in ((2 * p, psA), (2 * p + 1, psB)):
                    E.wait("vector", "te", ro_tok)
                    E.wait("vector", "x%d" % (ti % 2), xrtok[ti])
                    E.op("vector", lambda v, ti=ti, ps=ps: v.tensor_tensor(
                        AF[:, Y2:Y2 + D], xroll(ti), ps[:, 0:2048], ALU.add), "ve")
                    ytok = E.tok("ve")
                    if debug and ti == 0:
                        E.wait("sync", "ve", ytok)
                        E.dma("sync", dbg["y1"], AF[:, Y2:Y2 + D], "od")
                    if ti + 2 < NT:
                        E.wait("sync", "ve", ytok)
                        xrtok[ti + 2] = E.dma("sync", xroll(ti + 2), x_tiled[ti + 2], "x%d" % (ti % 2))
                    E.op("vector", lambda v: v.reduce_sum(
                        stats[:, 0:1], AF[:, Y2:Y2 + D], axis=AX), "ve")
                    E.wait("scalar", "ve", ytok)
                    E.op("scalar", lambda s: s.activation(
                        AF[:, ELU:ELU + 1024], AF[:, Y2:Y2 + 1024], ACTF.Square,
                        accum_out=stats[:, 1:2]), "se")
                    E.op("scalar", lambda s: s.activation(
                        AF[:, ELU:ELU + 1024], AF[:, Y2 + 1024:Y2 + 2048], ACTF.Square,
                        accum_out=stats[:, 2:3]), "se")
                    sqtok = E.tok("se")
                    E.wait("vector", "se", sqtok)
                    E.op("vector", lambda v: v.tensor_scalar(
                        stats[:, 3:4], stats[:, 0:1], 1.0 / D, None, ALU.mult), "ve")
                    E.op("vector", lambda v: v.tensor_add(
                        stats[:, 4:5], stats[:, 1:2], stats[:, 2:3]), "ve")
                    E.op("vector", lambda v: v.tensor_mul(
                        stats[:, 5:6], stats[:, 3:4], stats[:, 3:4]), "ve")
                    E.op("vector", lambda v: v.tensor_scalar(
                        stats[:, 9:10], stats[:, 4:5], 1.0 / D, None, ALU.mult), "ve")
                    E.wait("vector", "ve", E.tok("ve"))
                    E.op("vector", lambda v: v.tensor_sub(
                        stats[:, 6:7], stats[:, 9:10], stats[:, 5:6]), "ve")
                    E.wait("vector", "ve", E.tok("ve"))
                    E.op("vector", lambda v: v.tensor_scalar(
                        stats[:, 6:7], stats[:, 6:7], EPS_LN, None, ALU.add), "ve")
                    vtok = E.tok("ve")
                    E.wait("scalar", "ve", vtok)
                    E.op("scalar", lambda s: s.activation(
                        stats[:, 7:8], stats[:, 6:7], ACTF.Sqrt), "se")
                    sdtok = E.tok("se")
                    E.wait("vector", "se", sdtok)
                    E.op("vector", lambda v: v.reciprocal(stats[:, 8:9], stats[:, 7:8]), "ve")
                    E.wait("vector", "ve", E.tok("ve"))
                    E.op("vector", lambda v, ti=ti: v.tensor_scalar(
                        AB[:, X1T + ti * D: X1T + (ti + 1) * D], AF[:, Y2:Y2 + D],
                        stats[:, 3:4], stats[:, 8:9], ALU.subtract, ALU.mult), "ve")
                    x1_done[ti] = E.tok("ve")
                    if debug and ti == 0:
                        E.wait("sync", "ve", x1_done[ti])
                        E.dma("sync", dbg["st1"], stats[:, :], "od")
                ro_free = [x1_done[2 * p], x1_done[2 * p + 1]]
            tok_x1 = E.tok("ve")
            if debug:
                E.wait("sync", "ve", tok_x1)
                E.dma("sync", dbg["x1"], AB[:, X1T:X1T + 8 * D], "od")

            ckpt(8)
            def x1fm(dc):
                return AB[:, XFM + dc * 512: XFM + (dc + 1) * 512]

            def hgrp(i):
                return AB[:, XFM + 8192 + i * 512: XFM + 8192 + (i + 1) * 512]
            h2sb = AB[:, XFM + 12288: XFM + 12800]

            def wpc(i):
                return AB[:, QFM + i * D: QFM + (i + 1) * D]

            def ggu(buf):
                return AB[:, WS1 + buf * D: WS1 + (buf + 1) * D]

            def ugu(buf):
                return AB[:, WS2 + buf * D: WS2 + (buf + 1) * D]

            def sg(r):
                return AF[:, ELU + r * 512: ELU + (r + 1) * 512]

            def acc_t(tt):
                return AF[:, ACC + tt * D: ACC + (tt + 1) * D]

            qc_ap = qc_out.ap()

            def load_gu(gc, buf, which):
                blk = gc // 16
                col = (gc % 16) * 128
                row0 = blk * 3072
                dst = (ggu if which == "g" else ugu)(buf)
                sem = ("gg" if which == "g" else "gu") + str(buf)
                return E.dma("sync",
                             dst.rearrange("p (dc j) -> p dc j", j=128),
                             qc_ap[row0:row0 + D, col:col + 128]
                             .rearrange("(dc p) j -> p dc j", p=128), sem)

            def proj_rows(hc):
                rank = hc // 8
                local = hc * 128 - rank * 1024
                r0 = rank * 3072 + D + local
                return qc_ap[r0:r0 + 128, :]

            E.wait("sync", "cc1", tok_ag1)
            E.wait("tensor", "ve", tok_x1)
            out_toks = []
            for half in range(2):
                hb = half * 512
                t0 = half * 4
                te_now = E.tok("te")
                E.wait("vector", "te", te_now)
                E.wait("scalar", "te", te_now)
                E.wait("sync", "te", te_now)
                E.wait("sync", "ve", tok_x1)
                x1fm_tok = [0] * 16
                for dc in range(16):
                    for tt in range(4):
                        x1fm_tok[dc] = E.dma(
                            "sync", x1fm(dc)[:, tt * 128:(tt + 1) * 128],
                            AB[:, X1T + (t0 + tt) * D + dc * 128:
                                  X1T + (t0 + tt) * D + dc * 128 + 128],
                            "xf", transpose=True)
                tok_x1fm = E.tok("xf")
                E.wait("tensor", "xf", tok_x1fm)

                gtok = [0, 0]
                utok = [0, 0]
                gtok[0] = load_gu(0, 0, "g")
                utok[0] = load_gu(64, 0, "u")
                gtok[1] = load_gu(1, 1, "g")
                utok[1] = load_gu(65, 1, "u")
                wpct = [0] * 8
                gu_free = [0, 0]
                pj_free = [0, 0, 0]
                prev_grp_pj = 0
                h2_first = True
                for g in range(8):
                    for i in range(8):
                        hc = g * 8 + i
                        E.wait("sync", "te", prev_grp_pj)
                        wpct[i] = E.dma("sync", wpc(i), proj_rows(hc), "wp")
                    for i in range(8):
                        c = g * 8 + i
                        buf = c % 2
                        pr = c % 2
                        E.wait("tensor", "gg%d" % buf, gtok[buf])
                        E.wait("tensor", "gu%d" % buf, utok[buf])
                        E.wait("tensor", "ve", gu_free[pr])
                        for dc in range(16):
                            E.op("tensor", lambda te, pr=pr, buf=buf, dc=dc: te.matmul(
                                psA[:, pr * 1024:pr * 1024 + 512],
                                ggu(buf)[:, dc * 128:(dc + 1) * 128], x1fm(dc),
                                start=(dc == 0), stop=(dc == 15)), "te")
                        for dc in range(16):
                            E.op("tensor", lambda te, pr=pr, buf=buf, dc=dc: te.matmul(
                                psA[:, pr * 1024 + 512:pr * 1024 + 1024],
                                ugu(buf)[:, dc * 128:(dc + 1) * 128], x1fm(dc),
                                start=(dc == 0), stop=(dc == 15)), "te")
                        gu_mm = E.tok("te")
                        E.wait("scalar", "te", gu_mm)
                        E.op("scalar", lambda s, pr=pr: s.activation(
                            sg(pr), psA[:, pr * 1024:pr * 1024 + 512], ACTF.Silu), "se")
                        sg_tok = E.tok("se")
                        E.wait("vector", "se", sg_tok)
                        E.op("vector", lambda v, pr=pr, i=i: v.tensor_tensor(
                            hgrp(i), sg(pr), psA[:, pr * 1024 + 512:pr * 1024 + 1024],
                            ALU.mult), "ve")
                        gu_free[pr] = E.tok("ve")
                        h_tok = gu_free[pr]
                        E.wait("scalar", "ve", h_tok)
                        E.op("scalar", lambda s, i=i: s.activation(
                            h2sb, hgrp(i), ACTF.Square), "se")
                        h2_tok = E.tok("se")
                        E.wait("tensor", "se", h2_tok)
                        E.op("tensor", lambda te, first=h2_first, last=(c == 63): te.matmul(
                            psB[0:1, 0:512], ones_col, h2sb,
                            start=first, stop=last), "te")
                        h2_first = False
                        if c + 2 < 64:
                            E.wait("sync", "te", gu_mm)
                            gtok[buf] = load_gu(c + 2, buf, "g")
                            utok[buf] = load_gu(64 + c + 2, buf, "u")
                        if debug and half == 0 and c == 0:
                            E.wait("sync", "ve", h_tok)
                            E.dma("sync", dbg["h0"], hgrp(0), "od")
                    grp_h = E.tok("ve")
                    E.wait("tensor", "ve", grp_h)
                    E.wait("tensor", "wp", E.tok("wp"))
                    for tt in range(4):
                        for sl in range(4):
                            b = (tt * 4 + sl) % 3
                            E.wait("tensor", "ve", pj_free[b])
                            for i in range(8):
                                E.op("tensor", lambda te, b=b, i=i, tt=tt, sl=sl: te.matmul(
                                    psB[:, 512 + b * 512:512 + (b + 1) * 512],
                                    hgrp(i)[:, tt * 128:(tt + 1) * 128],
                                    wpc(i)[:, sl * 512:(sl + 1) * 512],
                                    start=(i == 0), stop=(i == 7)), "te")
                            pj_mm = E.tok("te")
                            E.wait("vector", "te", pj_mm)
                            if g == 0:
                                pj_free[b] = E.op("vector", lambda v, b=b, tt=tt, sl=sl: v.tensor_copy(
                                    acc_t(tt)[:, sl * 512:(sl + 1) * 512],
                                    psB[:, 512 + b * 512:512 + (b + 1) * 512]), "ve")
                            else:
                                pj_free[b] = E.op(
                                    "vector", lambda v, b=b, tt=tt, sl=sl: v.tensor_tensor(
                                        acc_t(tt)[:, sl * 512:(sl + 1) * 512],
                                        acc_t(tt)[:, sl * 512:(sl + 1) * 512],
                                        psB[:, 512 + b * 512:512 + (b + 1) * 512], ALU.add), "ve")
                    prev_grp_pj = E.tok("te")
                tok_acc = E.tok("ve")
                if debug and half == 0:
                    E.wait("sync", "ve", tok_acc)
                    E.dma("sync", dbg["acc0"], acc_t(0), "od")

                h2done = E.tok("te")
                E.wait("vector", "te", h2done)
                E.op("vector", lambda v: v.tensor_copy(h2row[0:1, :], psB[0:1, 0:512]), "ve")
                E.wait("vector", "ve", E.tok("ve"))
                E.op("vector", lambda v: v.tensor_scalar(
                    rqr_row[0:1, :], h2row[0:1, :], 1.0 / H_FFN, EPS_RMS, ALU.mult, ALU.add), "ve")
                t_h2 = E.tok("ve")
                E.wait("scalar", "ve", t_h2)
                E.op("scalar", lambda s: s.activation(
                    h2row[0:1, :], rqr_row[0:1, :], ACTF.Sqrt), "se")
                t_sq = E.tok("se")
                E.wait("vector", "se", t_sq)
                E.op("vector", lambda v: v.reciprocal(
                    rqr_row[0:1, :], h2row[0:1, :]), "ve")
                t_rq2 = E.tok("ve")
                E.wait("tensor", "ve", tok_acc)
                E.wait("tensor", "ve", t_rq2)
                for tt in range(4):
                    mt = E.op("tensor", lambda te, tt=tt: te.transpose(
                        psA[:, tt * 512:tt * 512 + 1],
                        rqr_row[0:1, tt * 128:(tt + 1) * 128], ident_s[0:1, 0:1]), "te")
                    E.wait("vector", "te", mt)
                    E.op("vector", lambda v, tt=tt, t0=t0: v.tensor_copy(
                        rqr_tm[:, t0 + tt:t0 + tt + 1], psA[:, tt * 512:tt * 512 + 1]), "ve")
                t_rqt = E.tok("ve")
                if debug and half == 0:
                    E.wait("sync", "ve", t_rq2)
                    E.dma("sync", dbg["rqr"][0:1, 0:512], rqr_row[0:1, :], "od")

                for tt in range(4):
                    ti = t0 + tt
                    E.wait("vector", "ve", t_rqt)
                    E.op("vector", lambda v, tt=tt, ti=ti: v.scalar_tensor_tensor(
                        AF[:, Y2:Y2 + D], acc_t(tt), rqr_tm[:, ti:ti + 1],
                        AB[:, X1T + ti * D: X1T + (ti + 1) * D], ALU.mult, ALU.add), "ve")
                    ytok = E.tok("ve")
                    E.op("vector", lambda v: v.reduce_sum(
                        stats[:, 0:1], AF[:, Y2:Y2 + D], axis=AX), "ve")
                    E.wait("scalar", "ve", ytok)
                    E.op("scalar", lambda s: s.activation(
                        AF[:, ELU:ELU + 1024], AF[:, Y2:Y2 + 1024], ACTF.Square,
                        accum_out=stats[:, 1:2]), "se")
                    E.op("scalar", lambda s: s.activation(
                        AF[:, ELU:ELU + 1024], AF[:, Y2 + 1024:Y2 + 2048], ACTF.Square,
                        accum_out=stats[:, 2:3]), "se")
                    sqtok = E.tok("se")
                    E.wait("vector", "se", sqtok)
                    E.op("vector", lambda v: v.tensor_scalar(
                        stats[:, 3:4], stats[:, 0:1], 1.0 / D, None, ALU.mult), "ve")
                    E.op("vector", lambda v: v.tensor_add(
                        stats[:, 4:5], stats[:, 1:2], stats[:, 2:3]), "ve")
                    E.op("vector", lambda v: v.tensor_mul(
                        stats[:, 5:6], stats[:, 3:4], stats[:, 3:4]), "ve")
                    E.op("vector", lambda v: v.tensor_scalar(
                        stats[:, 9:10], stats[:, 4:5], 1.0 / D, None, ALU.mult), "ve")
                    E.wait("vector", "ve", E.tok("ve"))
                    E.op("vector", lambda v: v.tensor_sub(
                        stats[:, 6:7], stats[:, 9:10], stats[:, 5:6]), "ve")
                    E.wait("vector", "ve", E.tok("ve"))
                    E.op("vector", lambda v: v.tensor_scalar(
                        stats[:, 6:7], stats[:, 6:7], EPS_LN, None, ALU.add), "ve")
                    vtok = E.tok("ve")
                    E.wait("scalar", "ve", vtok)
                    E.op("scalar", lambda s: s.activation(
                        stats[:, 7:8], stats[:, 6:7], ACTF.Sqrt), "se")
                    sdtok = E.tok("se")
                    E.wait("vector", "se", sdtok)
                    E.op("vector", lambda v: v.reciprocal(stats[:, 8:9], stats[:, 7:8]), "ve")
                    E.wait("vector", "ve", E.tok("ve"))
                    E.op("vector", lambda v: v.tensor_scalar(
                        AF[:, ELU:ELU + D], AF[:, Y2:Y2 + D],
                        stats[:, 3:4], stats[:, 8:9], ALU.subtract, ALU.mult), "ve")
                    otok = E.tok("ve")
                    E.wait("sync", "ve", otok)
                    out_toks.append(E.dma(
                        "sync", y_h.rearrange("(t p) d -> t p d", p=128)[ti],
                        AF[:, ELU:ELU + D], "od"))
                    E.wait("vector", "od", out_toks[-1])

            E.wait("sync", "od", E.tok("od"))
        except _Stop:
            pass


        block = stk.enter_context(nc.Block())

        def run(eng_name):
            def body(e):
                for fn in E.q[eng_name]:
                    fn(e)
            return body

        block.tensor(run("tensor"))
        block.vector(run("vector"))
        block.scalar(run("scalar"))
        block.sync(run("sync"))
        block.gpsimd(run("gpsimd"))

    return nc, dbg


_NC_CACHE = {}


def _get_nc(debug=False, stage=99):
    key = (bool(debug), stage)
    if key not in _NC_CACHE:
        _NC_CACHE[key] = build_nc(debug, stage)
    return _NC_CACHE[key]


def make_in_maps(inputs):
    f32 = np.float32
    x = np.asarray(inputs["L0_attn_x"], f32)
    wq = np.asarray(inputs["L0_attn_wq"], f32)
    bq = np.asarray(inputs["L0_attn_bq"], f32)
    wk = np.asarray(inputs["L0_attn_wk"], f32)
    bk = np.asarray(inputs["L0_attn_bk"], f32)
    wv = np.asarray(inputs["L0_attn_wv"], f32)
    bv = np.asarray(inputs["L0_attn_bv"], f32)
    wo = np.asarray(inputs["L0_attn_wo"], f32)
    bo = np.asarray(inputs["L0_attn_bo"], f32)
    wgu = np.asarray(inputs["L0_ffn_w_gate_up"], f32)
    wpj = np.asarray(inputs["L0_ffn_w_proj"], f32)

    xf = np.ascontiguousarray(x.reshape(2 * 4096, D))
    consts = np.zeros((128, 4), dtype=ml_dtypes.bfloat16)
    consts[:, 0] = 1.0
    consts[:64, 1] = 1.0
    consts[64:, 2] = 1.0
    sel2 = np.zeros((2, 128), dtype=ml_dtypes.bfloat16)
    sel2[0, :64] = 1.0
    sel2[1, 64:] = 1.0
    hsel = np.zeros((16, 16 * 128), dtype=f32)
    for h in range(16):
        hsel[h, h * 128:(h + 1) * 128] = 1.0
    shared = {
        "wqT": np.ascontiguousarray(wq.T), "wkT": np.ascontiguousarray(wk.T),
        "wvT": np.ascontiguousarray(wv.T), "woT": np.ascontiguousarray(wo.T),
        "bq": np.ascontiguousarray(bq.reshape(1, D)),
        "bk": np.ascontiguousarray(bk.reshape(1, D)),
        "bv": np.ascontiguousarray(bv.reshape(1, D)),
        "bo": np.ascontiguousarray(bo.reshape(1, D)),
        "ident": np.eye(128, dtype=f32),
        "identb": np.eye(128, dtype=ml_dtypes.bfloat16),
        "rowone": np.ones((1, 128), dtype=ml_dtypes.bfloat16),
        "consts_bf": consts, "sel2": sel2, "hsel": hsel,
        "sel2f": sel2.astype(f32),
    }
    in_maps = []
    for c in range(N_CORES):
        m = dict(shared)
        m["x"] = np.ascontiguousarray(xf[c * TOK:(c + 1) * TOK])
        m["wguT"] = np.ascontiguousarray(wgu[c * D:(c + 1) * D, :].T)
        m["wprojT"] = np.ascontiguousarray(wpj[:, c * 1024:(c + 1) * 1024].T)
        bm = np.zeros((128, 2), dtype=f32)
        bm[:, c // 4] = 1.0
        m["bmask"] = bm
        in_maps.append(m)
    return in_maps


def kernel(**inputs):
    nc, _dbg = _get_nc(debug=False)
    in_maps = make_in_maps(inputs)
    res = run_bass_kernel_spmd(nc, in_maps, core_ids=list(range(N_CORES)))
    out = np.concatenate([res.results[c]["y"] for c in range(N_CORES)], axis=0)
    return out.reshape(2, 4096, D).astype(np.float32)


# version 12 (final)
# speedup vs baseline: 1.0729x; 1.0729x over previous
import numpy as np
import ml_dtypes
import concourse.bass as bass
import concourse.mybir as mybir
from concourse.bass_utils import run_bass_kernel_spmd

F32 = mybir.dt.float32
BF16 = mybir.dt.bfloat16
AX = mybir.AxisListType.X
ALU = mybir.AluOpType
ACTF = mybir.ActivationFunctionType

N_CORES = 8
D = 2048
NH = 16
TOK = 1024
NT = TOK // 128
H_FFN = 8192
EPS_LN = 1e-5
EPS_RMS = float(np.finfo(np.float32).eps)
AGW = D + 16

ENGS = ("tensor", "vector", "scalar", "sync", "gpsimd")

XFM = 0
QFM = 16384
WS1 = 32768
WS2 = 36864
WS3 = 40960
KVB = 45056
X1T = 45056
QU = 53248
KVG = 62976
NBF = 65280

XR = 0
ACC = 4096
Y2 = 12288
ELU = 14336
NF32 = 16896


class _Stop(Exception):
    pass


class Emitter:
    def __init__(self):
        self.q = {e: [] for e in ENGS}
        self.cnt = {}
        self.sems = {}

    def new_sem(self, name, stk, nc):
        self.sems[name] = stk.enter_context(nc.semaphore(name))
        self.cnt[name] = 0

    def emit(self, eng, fn):
        self.q[eng].append(fn)

    def wait(self, eng, sem, tok):
        if tok and tok > 0:
            s = self.sems[sem]
            self.emit(eng, lambda e, s=s, tok=tok: e.wait_ge(s, tok))

    def op(self, eng, fn, sem=None, n=1):
        tok = None
        if sem is not None:
            self.cnt[sem] += n
            tok = self.cnt[sem]
            s = self.sems[sem]

            def wrapped(e, fn=fn, s=s, n=n):
                fn(e).then_inc(s, n)
            self.emit(eng, wrapped)
        else:
            self.emit(eng, lambda e, fn=fn: fn(e))
        return tok

    def dma(self, eng, out, in_, sem, **kw):
        return self.op(
            eng, lambda e, out=out, in_=in_, kw=kw: e.dma_start(out=out, in_=in_, **kw),
            sem=sem, n=16)

    def tok(self, sem):
        return self.cnt[sem]


def build_nc(debug=False, stage=99):
    nc = bass.Bass("TRN2", target_bir_lowering=False, detect_race_conditions=False)

    x_h = nc.dram_tensor("x", [TOK, D], F32, kind="ExternalInput").ap()
    wT_h = {n: nc.dram_tensor(n, [D, D], F32, kind="ExternalInput").ap()
            for n in ("wqT", "wkT", "wvT", "woT")}
    b_h = {n: nc.dram_tensor(n, [1, D], F32, kind="ExternalInput").ap()
           for n in ("bq", "bk", "bv", "bo")}
    wguT_h = nc.dram_tensor("wguT", [D, D], F32, kind="ExternalInput").ap()
    wprojT_h = nc.dram_tensor("wprojT", [1024, D], F32, kind="ExternalInput").ap()
    ident_h = nc.dram_tensor("ident", [128, 128], F32, kind="ExternalInput").ap()
    identb_h = nc.dram_tensor("identb", [128, 128], BF16, kind="ExternalInput").ap()
    consts_h = nc.dram_tensor("consts_bf", [128, 4], BF16, kind="ExternalInput").ap()
    sel2_h = nc.dram_tensor("sel2", [2, 128], BF16, kind="ExternalInput").ap()
    rowone_h = nc.dram_tensor("rowone", [1, 128], BF16, kind="ExternalInput").ap()
    hsel_h = nc.dram_tensor("hsel", [16, 16 * 128], F32, kind="ExternalInput").ap()
    bmask_h = nc.dram_tensor("bmask", [128, 2], F32, kind="ExternalInput").ap()
    y_h = nc.dram_tensor("y", [TOK, D], F32, kind="ExternalOutput").ap()

    qc_in = nc.dram_tensor("qc_in", [3072, D], BF16)
    qc_out = nc.dram_tensor("qc_out", [N_CORES * 3072, D], BF16)
    ag2_in = nc.dram_tensor("ag2_in", [128, AGW], F32)
    ag2_out = nc.dram_tensor("ag2_out", [N_CORES * 128, AGW], F32)

    dbg = {}
    if debug:
        for name, shape, dt in [
            ("x_fm", [128, 16 * TOK], BF16), ("q_fm", [128, 16 * TOK], BF16),
            ("kvp", [128, AGW], F32), ("kvg", [128, AGW], BF16),
            ("what_gu0", [128, D], BF16), ("what_pj0", [128, D], BF16),
            ("rq", [128, 8 * 16], F32), ("qksraw", [128, 8 * 16], F32), ("y1", [128, D], F32), ("st1", [128, 16], F32),
            ("attn", [128, 16 * TOK], BF16), ("x1", [128, 8 * D], BF16),
            ("h0", [128, 512], BF16), ("acc0", [128, D], F32),
            ("rqr", [1, TOK], F32),
        ]:
            dbg[name] = nc.dram_tensor("dbg_" + name, shape, dt, kind="ExternalOutput").ap()

    from contextlib import ExitStack
    with ExitStack() as stk:
        def sbt(name, shape, dt):
            return stk.enter_context(nc.sbuf_tensor(name, shape, dt))

        AB = sbt("AB", [128, NBF], BF16)
        AF = sbt("AF", [128, NF32], F32)
        ident_s = sbt("ident_s", [128, 128], F32)
        identb_s = sbt("identb_s", [128, 128], BF16)
        consts = sbt("consts_s", [128, 4], BF16)
        sel2 = sbt("sel2_s", [2, 128], BF16)
        rowone = sbt("rowone_s", [1, 128], BF16)
        bmask = sbt("bmask_s", [128, 2], F32)
        scT = sbt("scT", [2, D], BF16)
        stats = sbt("stats", [128, 16], F32)
        rq_tm = sbt("rq_tm", [128, 8 * 16], F32)
        qksraw = sbt("qksraw", [128, 8 * 16], F32)
        rqr_tm = sbt("rqr_tm", [128, 8], F32)
        bq_fm = sbt("bq_fm", [128, 16], F32)
        nbq_fm = sbt("nbq_fm", [128, 16], F32)
        bk_fm = sbt("bk_fm", [128, 16], F32)
        nbk_fm = sbt("nbk_fm", [128, 16], F32)
        bv_fm = sbt("bv_fm", [128, 16], F32)

        psA = stk.enter_context(nc.psum_tensor("psA", [128, 2048], F32))
        psB = stk.enter_context(nc.psum_tensor("psB", [128, 2048], F32))

        rq_fm = AF[0:16, ACC + 2064: ACC + 2064 + TOK]
        hsel = AF[0:16, ACC + 3088: ACC + 3088 + 16 * 128]
        h2row = AF[0:1, ELU + 1536: ELU + 2048]
        rqr_row = AF[0:1, ELU + 1024: ELU + 1536]
        borow = AB[0:1, WS2: WS2 + D]

        E = Emitter()
        for s in ("gd", "sd", "mi", "x0", "x1", "qw0", "qw1", "st0", "st1",
                  "wq0", "wq1", "wk0", "wk1", "wv0", "wv1", "wo0", "wo1",
                  "gg0", "gg1", "gu0", "gu1", "wp", "xf", "ag", "cc1", "cc2",
                  "ra", "rb", "rc", "rd",
                  "te", "ve", "se", "ge", "od"):
            E.new_sem(s, stk, nc)

        ones_col = consts[:, 0:1]
        gsel = consts[:, 1:3]

        def ckpt(k):
            if stage < k:
                raise _Stop()

        try:
            for out, in_ in [(ident_s[:], ident_h[:, :]), (identb_s[:], identb_h[:, :]),
                             (consts[:], consts_h[:, :]), (sel2[:], sel2_h[:, :]),
                             (rowone[:], rowone_h[:, :]),
                             (bmask[:], bmask_h[:, :])]:
                E.dma("sync", out, in_, "mi")
            E.dma("sync", bq_fm[:], b_h["bq"][0:1, :].rearrange("o (c p) -> (o p) c", p=128), "mi", allow_slow_non_contiguous=True)
            E.dma("sync", bk_fm[:], b_h["bk"][0:1, :].rearrange("o (c p) -> (o p) c", p=128), "mi", allow_slow_non_contiguous=True)
            E.dma("sync", bv_fm[:], b_h["bv"][0:1, :].rearrange("o (c p) -> (o p) c", p=128), "mi", allow_slow_non_contiguous=True)
            tok_bias = E.tok("mi")
            E.wait("vector", "mi", tok_bias)
            E.op("vector", lambda v: v.tensor_scalar(nbq_fm[:], bq_fm[:], -1.0, None, ALU.mult), "ve")
            E.op("vector", lambda v: v.tensor_scalar(nbk_fm[:], bk_fm[:], -1.0, None, ALU.mult), "ve")
            tok_nb = E.tok("ve")

            x_tiled = x_h.rearrange("(t p) d -> t p d", p=128)

            def xroll(i):
                return AF[:, XR + (i % 2) * D: XR + (i % 2) * D + D]

            xtok = [0] * NT
            xtok[0] = E.dma("sync", xroll(0), x_tiled[0], "x0")
            xtok[1] = E.dma("sync", xroll(1), x_tiled[1], "x1")

            E.wait("tensor", "mi", tok_bias)
            tp_evac = [0, 0, 0, 0]
            for t in range(NT):
                E.wait("tensor", "x%d" % (t % 2), xtok[t])
                for dc in range(16):
                    r = dc % 4
                    E.wait("tensor", "ve", tp_evac[r])
                    mt = E.op("tensor", lambda te, r=r, t=t, dc=dc: te.transpose(
                        psA[:, r * 512:r * 512 + 128],
                        xroll(t)[:, dc * 128:dc * 128 + 128], ident_s[:]), "te")
                    E.wait("vector", "te", mt)
                    tp_evac[r] = E.op("vector", lambda v, r=r, t=t, dc=dc: v.tensor_copy(
                        AB[:, XFM + dc * TOK + t * 128: XFM + dc * TOK + t * 128 + 128],
                        psA[:, r * 512:r * 512 + 128]), "ve")
                if t + 2 < NT:
                    E.wait("sync", "te", E.tok("te"))
                    xtok[t + 2] = E.dma("sync", xroll(t + 2), x_tiled[t + 2], "x%d" % (t % 2))
            tok_xfm_ve = E.tok("ve")
            if debug:
                E.wait("sync", "ve", tok_xfm_ve)
                E.dma("sync", dbg["x_fm"], AB[:, XFM:XFM + 16 * TOK], "od")

            ckpt(2)

            ckpt(3)
            elu_t = AF[:, ELU: ELU + 512]
            elu_e = AF[:, ELU + 512: ELU + 1024]
            elu_s = AF[:, ELU + 1024: ELU + 1536]

            def load_wcol(w_ap, fc, buf, base, sem):
                return E.dma(
                    "gpsimd",
                    AB[:, base + buf * D: base + (buf + 1) * D].rearrange("p (dc j) -> p dc j", j=128),
                    w_ap[:, fc * 128:(fc + 1) * 128].rearrange("(dc p) j -> p dc j", p=128), sem)

            elu_guard = [0]

            def emit_elu_epilogue(psum_ap, bias_col, nbias_col, dst_ap):
                E.wait("scalar", "ve", elu_guard[0])
                E.op("scalar", lambda s: s.activation(
                    elu_t[:], psum_ap, ACTF.Relu, bias=nbias_col, scale=-1.0), "se")
                ttok = E.tok("se")
                E.op("scalar", lambda s: s.activation(
                    elu_e[:], elu_t[:], ACTF.Exp, scale=-1.0), "se")
                etok = E.tok("se")
                E.wait("vector", "se", ttok)
                E.op("vector", lambda v: v.scalar_tensor_tensor(
                    elu_s[:], psum_ap, bias_col, elu_t[:], ALU.add, ALU.add), "ve")
                ve1 = E.tok("ve")
                E.wait("vector", "se", etok)
                E.op("vector", lambda v: v.tensor_tensor(
                    dst_ap, elu_e[:], elu_s[:], ALU.add), "ve")
                elu_guard[0] = E.tok("ve")
                return ve1, elu_guard[0]

            wqtok = [0, 0]
            wqtok[0] = load_wcol(wT_h["wqT"], 0, 0, WS1, "wq0")
            wqtok[1] = load_wcol(wT_h["wqT"], 1, 1, WS1, "wq1")
            q_evac_ve = [0, 0]
            q_evac_se = [0, 0]
            E.wait("tensor", "ve", tok_xfm_ve)
            for fc in range(16):
                buf = fc % 2
                E.wait("tensor", "wq%d" % buf, wqtok[buf])
                for half in range(2):
                    r = half
                    E.wait("tensor", "ve", q_evac_ve[r])
                    E.wait("tensor", "se", q_evac_se[r])
                    for dc in range(16):
                        E.op("tensor", lambda te, r=r, buf=buf, dc=dc, half=half: te.matmul(
                            psA[:, r * 512:(r + 1) * 512],
                            AB[:, WS1 + buf * D + dc * 128: WS1 + buf * D + dc * 128 + 128],
                            AB[:, XFM + dc * TOK + half * 512: XFM + dc * TOK + half * 512 + 512],
                            start=(dc == 0), stop=(dc == 15)), "te")
                    mtok = E.tok("te")
                    E.wait("scalar", "te", mtok)
                    E.wait("scalar", "ve", tok_nb)
                    ve1, ve2 = emit_elu_epilogue(
                        psA[:, r * 512:(r + 1) * 512], bq_fm[:, fc:fc + 1], nbq_fm[:, fc:fc + 1],
                        AB[:, QFM + fc * TOK + half * 512: QFM + fc * TOK + half * 512 + 512])
                    q_evac_ve[r] = ve1
                    q_evac_se[r] = E.tok("se")
                if fc + 2 < 16:
                    E.wait("gpsimd", "te", E.tok("te"))
                    wqtok[buf] = load_wcol(wT_h["wqT"], fc + 2, buf, WS1, "wq%d" % buf)
            tok_qdone = E.tok("ve")
            if debug:
                E.wait("sync", "ve", tok_qdone)
                E.dma("sync", dbg["q_fm"], AB[:, QFM:QFM + 16 * TOK], "od")

            ckpt(4)
            def kfm(b):
                return AF[:, ACC + 4096 + b * 1024: ACC + 4096 + (b + 1) * 1024]

            def vfm(b):
                return AF[:, ACC + 6144 + b * 1024: ACC + 6144 + (b + 1) * 1024]

            def ktm(b):
                return AB[:, KVB + b * 1024: KVB + (b + 1) * 1024]

            def vtm(b):
                return AB[:, KVB + 2048 + b * 1024: KVB + 2048 + (b + 1) * 1024]

            wktok = [0, 0]
            wvtok = [0, 0]
            wktok[0] = load_wcol(wT_h["wkT"], 0, 0, WS2, "wk0")
            wvtok[0] = load_wcol(wT_h["wvT"], 0, 0, WS3, "wv0")
            wktok[1] = load_wcol(wT_h["wkT"], 1, 1, WS2, "wk1")
            wvtok[1] = load_wcol(wT_h["wvT"], 1, 1, WS3, "wv1")


            kv_evac = [0, 0]
            kv_se = [0, 0]
            tp_free = [0, 0]
            kvslot_free = [0, 0]
            buf_free = [0, 0]
            TPRE = [0, 512]
            SLOT = [1024, 1536]

            for h in range(NH):
                bb = h % 2
                E.wait("vector", "te", buf_free[bb])
                E.wait("scalar", "te", buf_free[bb])
                E.wait("tensor", "wk%d" % bb, wktok[bb])
                for half in range(2):
                    r = 2 + half
                    E.wait("tensor", "ve", kv_evac[half])
                    E.wait("tensor", "se", kv_se[half])
                    for dc in range(16):
                        E.op("tensor", lambda te, r=r, bb=bb, dc=dc, half=half: te.matmul(
                            psA[:, r * 512:(r + 1) * 512],
                            AB[:, WS2 + bb * D + dc * 128: WS2 + bb * D + dc * 128 + 128],
                            AB[:, XFM + dc * TOK + half * 512: XFM + dc * TOK + half * 512 + 512],
                            start=(dc == 0), stop=(dc == 15)), "te")
                    mtok = E.tok("te")
                    E.wait("scalar", "te", mtok)
                    ve1, _ = emit_elu_epilogue(
                        psA[:, r * 512:(r + 1) * 512], bk_fm[:, h:h + 1], nbk_fm[:, h:h + 1],
                        kfm(bb)[:, half * 512:(half + 1) * 512])
                    kv_evac[half] = ve1
                    kv_se[half] = E.tok("se")
                kfm_tok = E.tok("ve")
                E.op("vector", lambda v, h=h, bb=bb: v.reduce_sum(
                    AF[:, ACC + 2048 + h: ACC + 2048 + h + 1], kfm(bb)[:, :], axis=AX), "ve")
                for t in range(NT):
                    rr = t % 2
                    E.wait("tensor", "ve", kfm_tok)
                    E.wait("tensor", "se", tp_free[rr])
                    mt = E.op("tensor", lambda te, rr=rr, bb=bb, t=t: te.transpose(
                        psB[:, TPRE[rr]:TPRE[rr] + 128],
                        kfm(bb)[:, t * 128:(t + 1) * 128], ident_s[:]), "te")
                    E.wait("scalar", "te", mt)
                    tp_free[rr] = E.op("scalar", lambda s, rr=rr, bb=bb, t=t: s.copy(
                        ktm(bb)[:, t * 128:(t + 1) * 128], psB[:, TPRE[rr]:TPRE[rr] + 128]), "se")
                ktm_tok = E.tok("se")
                E.wait("tensor", "wv%d" % bb, wvtok[bb])
                for half in range(2):
                    r = 2 + half
                    E.wait("tensor", "ve", kv_evac[half])
                    E.wait("tensor", "se", kv_se[half])
                    for dc in range(16):
                        E.op("tensor", lambda te, r=r, bb=bb, dc=dc, half=half: te.matmul(
                            psA[:, r * 512:(r + 1) * 512],
                            AB[:, WS3 + bb * D + dc * 128: WS3 + bb * D + dc * 128 + 128],
                            AB[:, XFM + dc * TOK + half * 512: XFM + dc * TOK + half * 512 + 512],
                            start=(dc == 0), stop=(dc == 15)), "te")
                    mtok = E.tok("te")
                    E.wait("vector", "te", mtok)
                    E.op("vector", lambda v, r=r, h=h, bb=bb, half=half: v.tensor_scalar(
                        vfm(bb)[:, half * 512:(half + 1) * 512],
                        psA[:, r * 512:(r + 1) * 512], bv_fm[:, h:h + 1], None, ALU.add), "ve")
                    kv_evac[half] = E.tok("ve")
                vfm_tok = E.tok("ve")
                for t in range(NT):
                    rr = t % 2
                    E.wait("tensor", "ve", vfm_tok)
                    E.wait("tensor", "se", tp_free[rr])
                    mt = E.op("tensor", lambda te, rr=rr, bb=bb, t=t: te.transpose(
                        psB[:, TPRE[rr]:TPRE[rr] + 128],
                        vfm(bb)[:, t * 128:(t + 1) * 128], ident_s[:]), "te")
                    E.wait("scalar", "te", mt)
                    tp_free[rr] = E.op("scalar", lambda s, rr=rr, bb=bb, t=t: s.copy(
                        vtm(bb)[:, t * 128:(t + 1) * 128], psB[:, TPRE[rr]:TPRE[rr] + 128]), "se")
                vtm_tok = E.tok("se")
                slot = h % 2
                E.wait("tensor", "se", ktm_tok)
                E.wait("tensor", "se", vtm_tok)
                E.wait("tensor", "ve", kvslot_free[slot])
                for t in range(NT):
                    E.op("tensor", lambda te, slot=slot, bb=bb, t=t: te.matmul(
                        psB[:, SLOT[slot]:SLOT[slot] + 128],
                        ktm(bb)[:, t * 128:(t + 1) * 128],
                        vtm(bb)[:, t * 128:(t + 1) * 128],
                        start=(t == 0), stop=(t == NT - 1)), "te")
                kv_mm = E.tok("te")
                buf_free[bb] = kv_mm
                E.wait("vector", "te", kv_mm)
                kvslot_free[slot] = E.op("vector", lambda v, slot=slot, h=h: v.tensor_copy(
                    AF[:, ACC + h * 128: ACC + (h + 1) * 128],
                    psB[:, SLOT[slot]:SLOT[slot] + 128]), "ve")
                if h + 2 < NH:
                    E.wait("gpsimd", "te", kv_mm)
                    wktok[bb] = load_wcol(wT_h["wkT"], h + 2, bb, WS2, "wk%d" % bb)
                    wvtok[bb] = load_wcol(wT_h["wvT"], h + 2, bb, WS3, "wv%d" % bb)

            tok_kv_all = E.tok("ve")
            E.wait("sync", "ve", tok_kv_all)
            tok_ag2in = E.dma("sync", ag2_in.ap()[:, :], AF[:, ACC:ACC + AGW], "ag")
            if debug:
                E.dma("sync", dbg["kvp"], AF[:, ACC:ACC + AGW], "od")
            E.wait("gpsimd", "ag", tok_ag2in)
            E.op("gpsimd", lambda g: g.collective_compute(
                "AllGather", ALU.bypass, replica_groups=[list(range(N_CORES))],
                ins=[ag2_in.ap().opt()], outs=[ag2_out.ap().opt()]), "cc2")
            tok_ag2 = E.tok("cc2")

            def qw(hh):
                return AB[:, QU + hh * 1024: QU + (hh + 1) * 1024]
            aw = AB[:, QU + 2048: QU + 3072]

            def ssb(r):
                return AB[:, QU + 3072 + r * 512: QU + 3072 + (r + 1) * 512]
            rt = AB[:, QU + 4096: QU + 4608]
            qn_t = AB[:, QU + 4608: QU + 5120]
            qq_t = AB[:, QU + 5120: QU + 5632]

            def what(g):
                return AB[:, QU + 5632 + g * 2048: QU + 5632 + (g + 1) * 2048]

            def srec(r):
                return AF[:, Y2 + 1024 + r * 512: Y2 + 1024 + (r + 1) * 512]

            _qs = {"bc_ve": [0, 0], "bc_se": [0, 0], "sc": [0], "st": [0, 0]}

            def emit_quant(src_h, n_chunks, out_row0, dbg_key=None):
                E.wait("gpsimd", "ve", E.tok("ve"))
                qload = {(0, 0): E.dma("gpsimd", qw(0), src_h[0:128, 0:1024], "qw0")}
                bc_free_ve = _qs["bc_ve"]
                bc_free_se = _qs["bc_se"]
                st_free = _qs["st"]
                for c in range(n_chunks):
                    for hh in range(2):
                        ld = qload.pop((c, hh))
                        E.wait("scalar", "qw%d" % hh, ld)
                        E.wait("scalar", "te", E.tok("te"))
                        E.op("scalar", lambda s, hh=hh: s.activation(
                            aw[:], qw(hh), ACTF.Abs), "se")
                        awtok = E.tok("se")
                        sctoks = []
                        for sl in range(2):
                            E.wait("tensor", "se", awtok)
                            E.wait("tensor", "ve", _qs["sc"][0])
                            mt = E.op("tensor", lambda te, sl=sl: te.matmul(
                                psB[0:2, 1536:2048], gsel, aw[:, sl * 512:(sl + 1) * 512],
                                start=True, stop=True), "te")
                            E.wait("vector", "te", mt)
                            _qs["sc"][0] = E.op("vector", lambda v, hh=hh, sl=sl: v.tensor_scalar(
                                scT[0:2, hh * 1024 + sl * 512: hh * 1024 + (sl + 1) * 512],
                                psB[0:2, 1536:2048], 1.0 / 64.0, 1e-8, ALU.mult, ALU.max), "ve")
                        sctok = _qs["sc"][0]
                        nxt = (c, 1) if hh == 0 else (c + 1, 0)
                        if nxt[0] < n_chunks:
                            E.wait("gpsimd", "ve", E.tok("ve"))
                            qload[nxt] = E.dma("gpsimd", qw(nxt[1]),
                                               src_h[nxt[0] * 128:(nxt[0] + 1) * 128,
                                                     nxt[1] * 1024:(nxt[1] + 1) * 1024],
                                               "qw%d" % nxt[1])
                        for sl in range(2):
                            col = hh * 1024 + sl * 512
                            r = sl
                            E.wait("tensor", "ve", bc_free_ve[r])
                            E.wait("tensor", "se", bc_free_se[r])
                            E.wait("tensor", "ve", sctok)
                            bt = E.op("tensor", lambda te, col=col, r=r: te.matmul(
                                psB[:, r * 512:(r + 1) * 512], sel2[:],
                                scT[0:2, col:col + 512], start=True, stop=True), "te")
                            E.wait("vector", "te", bt)
                            E.op("vector", lambda v, r=r: v.reciprocal(
                                srec(r), psB[:, r * 512:(r + 1) * 512]), "ve")
                            bc_free_ve[r] = E.tok("ve")
                            E.wait("scalar", "te", bt)
                            E.wait("scalar", "ve", bc_free_ve[r])
                            bc_free_se[r] = E.op("scalar", lambda s, r=r: s.copy(
                                ssb(r), psB[:, r * 512:(r + 1) * 512]), "se")
                            E.op("vector", lambda v, hh=hh, sl=sl, r=r: v.tensor_tensor(
                                rt[:], qw(hh)[:, sl * 512:(sl + 1) * 512], srec(r), ALU.mult), "ve")
                            E.op("vector", lambda v: v.tensor_scalar(
                                qn_t[:], rt[:], -0.5, None, ALU.is_lt), "ve")
                            E.op("vector", lambda v: v.scalar_tensor_tensor(
                                qq_t[:], rt[:], 0.5, qn_t[:], ALU.is_gt, ALU.subtract), "ve")
                            stg = c % 2
                            E.wait("vector", "se", bc_free_se[r])
                            E.wait("vector", "st%d" % stg, st_free[stg])
                            E.op("vector", lambda v, stg=stg, col=col, r=r: v.tensor_tensor(
                                what(stg)[:, col:col + 512], ssb(r), qq_t[:], ALU.mult), "ve")
                    stg = c % 2
                    wtok = E.tok("ve")
                    E.wait("sync", "ve", wtok)
                    st_free[stg] = E.dma(
                        "sync",
                        qc_in.ap()[out_row0 + c * 128: out_row0 + (c + 1) * 128, :],
                        what(stg), "st%d" % stg)
                    if debug and dbg_key and c == 0:
                        E.dma("sync", dbg[dbg_key], what(stg), "od")
                return None

            emit_quant(wguT_h, 16, 0, "what_gu0")
            emit_quant(wprojT_h, 8, D, "what_pj0")
            tok_st0, tok_st1 = E.tok("st0"), E.tok("st1")
            tok_q_ve = E.tok("ve")
            tok_q_se = E.tok("se")

            ckpt(2.2)
            E.wait("gpsimd", "st0", tok_st0)
            E.wait("gpsimd", "st1", tok_st1)
            E.op("gpsimd", lambda g: g.collective_compute(
                "AllGather", ALU.bypass, replica_groups=[list(range(N_CORES))],
                ins=[qc_in.ap().opt()],
                outs=[qc_out.ap().opt()]), "cc1")
            tok_ag1 = E.tok("cc1")

            ckpt(5)
            kvb0 = AF[:, ACC + 2064: ACC + 2064 + AGW]
            kvb1 = AF[:, ACC + 4128: ACC + 4128 + AGW]
            tmpa = AF[:, ACC: ACC + AGW]
            tmpb = AF[:, ACC + 6192: ACC + 6192 + AGW]
            agb = ag2_out.ap()
            E.wait("sync", "cc2", tok_ag2)
            t0 = E.dma("sync", kvb0, agb[0 * 128:1 * 128, :], "ra")
            t1 = E.dma("sync", kvb1, agb[4 * 128:5 * 128, :], "rb")
            tadd0 = tadd1 = 0
            for j in range(3):
                E.wait("sync", "ve", tadd0)
                ta = E.dma("sync", tmpa, agb[(1 + j) * 128:(2 + j) * 128, :], "rc")
                E.wait("sync", "ve", tadd1)
                tb = E.dma("sync", tmpb, agb[(5 + j) * 128:(6 + j) * 128, :], "rd")
                if j == 0:
                    E.wait("vector", "ra", t0)
                    E.wait("vector", "rb", t1)
                E.wait("vector", "rc", ta)
                E.op("vector", lambda v: v.tensor_tensor(kvb0, kvb0, tmpa, ALU.add), "ve")
                tadd0 = E.tok("ve")
                E.wait("vector", "rd", tb)
                E.op("vector", lambda v: v.tensor_tensor(kvb1, kvb1, tmpb, ALU.add), "ve")
                tadd1 = E.tok("ve")
            E.op("vector", lambda v: v.tensor_scalar(
                tmpb, kvb0, bmask[:, 0:1], None, ALU.mult), "ve")
            E.op("vector", lambda v: v.scalar_tensor_tensor(
                AB[:, KVG:KVG + AGW], kvb1, bmask[:, 1:2], tmpb, ALU.mult, ALU.add), "ve")
            tok_kvg = E.tok("ve")
            kvg = AB[:, KVG:KVG + AGW]
            E.wait("sync", "ve", tok_kvg)
            tok_hsel = E.dma("sync", hsel, hsel_h[:, :], "mi")
            E.wait("gpsimd", "ve", tok_q_ve)
            tok_borow = E.dma("gpsimd", borow, b_h["bo"][0:1, :], "gd")
            if debug:
                E.wait("sync", "ve", tok_kvg)
                E.dma("sync", dbg["kvg"], kvg, "od")

            ckpt(6)
            def qks_ps(t):
                return psB[:, 1024 + (t % 2) * 512: 1024 + (t % 2) * 512 + 16]

            qks_free = [0, 0]
            rqT_free = [0, 0, 0, 0]
            E.wait("tensor", "ve", tok_kvg)
            E.wait("tensor", "se", E.tok("se"))
            for t in range(NT):
                E.wait("tensor", "ve", qks_free[t % 2])
                for h in range(NH):
                    E.op("tensor", lambda te, t=t, h=h: te.matmul(
                        qks_ps(t)[:, h:h + 1],
                        AB[:, QFM + h * TOK + t * 128: QFM + h * TOK + t * 128 + 128],
                        kvg[:, 2048 + h:2048 + h + 1],
                        start=True, stop=True), "te")
                mtok = E.tok("te")
                E.wait("vector", "te", mtok)
                E.op("vector", lambda v, t=t: v.tensor_scalar(
                    qksraw[:, t * 16:(t + 1) * 16], qks_ps(t), 1e-8, None, ALU.add), "ve")
                qks_free[t % 2] = E.tok("ve")
                E.wait("scalar", "ve", qks_free[t % 2])
                E.op("scalar", lambda s, t=t: s.activation(
                    stats[:, 0:16], qksraw[:, t * 16:(t + 1) * 16], ACTF.Ln), "se")
                E.op("scalar", lambda s, t=t: s.activation(
                    rq_tm[:, t * 16:(t + 1) * 16], stats[:, 0:16], ACTF.Exp, scale=-1.0), "se")
                rcp_tok = E.tok("se")
                rr = t % 4
                E.wait("tensor", "se", rcp_tok)
                E.wait("tensor", "ve", rqT_free[rr])
                mt = E.op("tensor", lambda te, rr=rr, t=t: te.transpose(
                    psA[0:16, rr * 512:rr * 512 + 128],
                    rq_tm[:, t * 16:(t + 1) * 16], ident_s[:]), "te")
                E.wait("vector", "te", mt)
                rqT_free[rr] = E.op("vector", lambda v, rr=rr, t=t: v.tensor_copy(
                    rq_fm[:, t * 128:(t + 1) * 128], psA[0:16, rr * 512:rr * 512 + 128]), "ve")
            tok_rq = E.tok("ve")
            if debug:
                E.wait("sync", "ve", tok_rq)
                E.dma("sync", dbg["rq"], rq_tm[:, :], "od")
                E.dma("sync", dbg["qksraw"], qksraw[:, :], "od")

            qkv_free = [0, 0]
            rqb_free = [0, 0]

            def rqb_sb(r):
                return AF[:, ELU + 1024 + r * 512: ELU + 1024 + (r + 1) * 512]

            E.wait("tensor", "ve", tok_rq)
            E.wait("tensor", "mi", tok_hsel)
            for h in range(NH):
                for half in range(2):
                    r = half
                    E.wait("tensor", "ve", qkv_free[r])
                    E.op("tensor", lambda te, h=h, half=half, r=r: te.matmul(
                        psA[:, r * 512:(r + 1) * 512],
                        kvg[:, h * 128:(h + 1) * 128],
                        AB[:, QFM + h * TOK + half * 512: QFM + h * TOK + half * 512 + 512],
                        start=True, stop=True), "te")
                    qkv_tok = E.tok("te")
                    E.wait("tensor", "se", rqb_free[r])
                    E.op("tensor", lambda te, h=h, half=half, r=r: te.matmul(
                        psA[:, 1024 + r * 512:1024 + (r + 1) * 512],
                        hsel[:, h * 128:(h + 1) * 128],
                        rq_fm[:, half * 512:(half + 1) * 512],
                        start=True, stop=True), "te")
                    rqb_tok = E.tok("te")
                    E.wait("scalar", "te", rqb_tok)
                    rqb_free[r] = E.op("scalar", lambda s, r=r: s.copy(
                        rqb_sb(r), psA[:, 1024 + r * 512:1024 + (r + 1) * 512]), "se")
                    E.wait("vector", "te", qkv_tok)
                    E.wait("vector", "se", rqb_free[r])
                    qkv_free[r] = E.op("vector", lambda v, h=h, half=half, r=r: v.tensor_tensor(
                        AB[:, XFM + h * TOK + half * 512: XFM + h * TOK + half * 512 + 512],
                        psA[:, r * 512:(r + 1) * 512], rqb_sb(r), ALU.mult), "ve")
            tok_attn = E.tok("ve")
            if debug:
                E.wait("sync", "ve", tok_attn)
                E.dma("sync", dbg["attn"], AB[:, XFM:XFM + 16 * TOK], "od")

            ckpt(7)
            def load_wo(dc, buf):
                return E.dma("gpsimd", AB[:, WS1 + buf * D: WS1 + (buf + 1) * D],
                             wT_h["woT"][dc * 128:(dc + 1) * 128, :], "wo%d" % buf)

            wotok = [load_wo(0, 0), load_wo(1, 1)]
            ro_free = [0, 0]
            xrtok = [0] * NT
            E.wait("tensor", "ve", tok_attn)
            E.wait("tensor", "ve", tok_q_ve)
            E.wait("tensor", "se", tok_q_se)
            E.wait("tensor", "gd", E.tok("gd"))
            xrtok[0] = E.dma("sync", xroll(0), x_tiled[0], "x0")
            xrtok[1] = E.dma("sync", xroll(1), x_tiled[1], "x1")
            x1_done = [0] * NT
            for p in range(4):
                E.wait("tensor", "ve", ro_free[0])
                E.wait("tensor", "ve", ro_free[1])
                for dc in range(16):
                    buf = dc % 2
                    E.wait("tensor", "wo%d" % buf, wotok[buf])
                    for ti, ps in ((2 * p, psA), (2 * p + 1, psB)):
                        for sl in range(4):
                            E.op("tensor", lambda te, ti=ti, ps=ps, dc=dc, buf=buf, sl=sl: te.matmul(
                                ps[:, sl * 512:(sl + 1) * 512],
                                AB[:, XFM + dc * TOK + ti * 128: XFM + dc * TOK + ti * 128 + 128],
                                AB[:, WS1 + buf * D + sl * 512: WS1 + buf * D + sl * 512 + 512],
                                start=(dc == 0), stop=False), "te")
                    if dc + 2 < 16:
                        E.wait("gpsimd", "te", E.tok("te"))
                        wotok[buf] = load_wo(dc + 2, buf)
                    elif p < 3:
                        E.wait("gpsimd", "te", E.tok("te"))
                        wotok[buf] = load_wo(dc + 2 - 16, buf)
                for ti, ps in ((2 * p, psA), (2 * p + 1, psB)):
                    for sl in range(4):
                        E.op("tensor", lambda te, ps=ps, sl=sl: te.matmul(
                            ps[:, sl * 512:(sl + 1) * 512], rowone[:],
                            borow[0:1, sl * 512:(sl + 1) * 512], start=False, stop=True), "te")
                ro_tok = E.tok("te")
                for ti, ps in ((2 * p, psA), (2 * p + 1, psB)):
                    E.wait("vector", "te", ro_tok)
                    E.wait("vector", "x%d" % (ti % 2), xrtok[ti])
                    E.op("vector", lambda v, ti=ti, ps=ps: v.tensor_tensor(
                        AF[:, Y2:Y2 + D], xroll(ti), ps[:, 0:2048], ALU.add), "ve")
                    ytok = E.tok("ve")
                    if debug and ti == 0:
                        E.wait("sync", "ve", ytok)
                        E.dma("sync", dbg["y1"], AF[:, Y2:Y2 + D], "od")
                    if ti + 2 < NT:
                        E.wait("sync", "ve", ytok)
                        xrtok[ti + 2] = E.dma("sync", xroll(ti + 2), x_tiled[ti + 2], "x%d" % (ti % 2))
                    E.op("vector", lambda v: v.reduce_sum(
                        stats[:, 0:1], AF[:, Y2:Y2 + D], axis=AX), "ve")
                    E.wait("scalar", "ve", ytok)
                    E.op("scalar", lambda s: s.activation(
                        AF[:, ELU:ELU + 1024], AF[:, Y2:Y2 + 1024], ACTF.Square,
                        accum_out=stats[:, 1:2]), "se")
                    E.op("scalar", lambda s: s.activation(
                        AF[:, ELU:ELU + 1024], AF[:, Y2 + 1024:Y2 + 2048], ACTF.Square,
                        accum_out=stats[:, 2:3]), "se")
                    sqtok = E.tok("se")
                    E.wait("vector", "se", sqtok)
                    E.op("vector", lambda v: v.tensor_scalar(
                        stats[:, 3:4], stats[:, 0:1], 1.0 / D, None, ALU.mult), "ve")
                    E.op("vector", lambda v: v.tensor_add(
                        stats[:, 4:5], stats[:, 1:2], stats[:, 2:3]), "ve")
                    E.op("vector", lambda v: v.tensor_mul(
                        stats[:, 5:6], stats[:, 3:4], stats[:, 3:4]), "ve")
                    E.op("vector", lambda v: v.tensor_scalar(
                        stats[:, 9:10], stats[:, 4:5], 1.0 / D, None, ALU.mult), "ve")
                    E.wait("vector", "ve", E.tok("ve"))
                    E.op("vector", lambda v: v.tensor_sub(
                        stats[:, 6:7], stats[:, 9:10], stats[:, 5:6]), "ve")
                    E.wait("vector", "ve", E.tok("ve"))
                    E.op("vector", lambda v: v.tensor_scalar(
                        stats[:, 6:7], stats[:, 6:7], EPS_LN, None, ALU.add), "ve")
                    vtok = E.tok("ve")
                    E.wait("scalar", "ve", vtok)
                    E.op("scalar", lambda s: s.activation(
                        stats[:, 7:8], stats[:, 6:7], ACTF.Sqrt), "se")
                    sdtok = E.tok("se")
                    E.wait("vector", "se", sdtok)
                    E.op("vector", lambda v: v.reciprocal(stats[:, 8:9], stats[:, 7:8]), "ve")
                    E.wait("vector", "ve", E.tok("ve"))
                    E.op("vector", lambda v, ti=ti: v.tensor_scalar(
                        AB[:, X1T + ti * D: X1T + (ti + 1) * D], AF[:, Y2:Y2 + D],
                        stats[:, 3:4], stats[:, 8:9], ALU.subtract, ALU.mult), "ve")
                    x1_done[ti] = E.tok("ve")
                    if debug and ti == 0:
                        E.wait("sync", "ve", x1_done[ti])
                        E.dma("sync", dbg["st1"], stats[:, :], "od")
                ro_free = [x1_done[2 * p], x1_done[2 * p + 1]]
            tok_x1 = E.tok("ve")
            if debug:
                E.wait("sync", "ve", tok_x1)
                E.dma("sync", dbg["x1"], AB[:, X1T:X1T + 8 * D], "od")

            ckpt(8)
            def x1fm(dc):
                return AB[:, XFM + dc * 512: XFM + (dc + 1) * 512]

            def hgrp(i):
                return AB[:, XFM + 8192 + i * 512: XFM + 8192 + (i + 1) * 512]
            h2sb = AB[:, XFM + 12288: XFM + 12800]

            def wpc(i):
                return AB[:, QFM + i * D: QFM + (i + 1) * D]

            def ggu(buf):
                return AB[:, WS1 + buf * D: WS1 + (buf + 1) * D]

            def ugu(buf):
                return AB[:, WS2 + buf * D: WS2 + (buf + 1) * D]

            def sg(r):
                return AF[:, ELU + r * 512: ELU + (r + 1) * 512]

            def acc_t(tt):
                return AF[:, ACC + tt * D: ACC + (tt + 1) * D]

            qc_ap = qc_out.ap()

            def load_gu(gc, buf, which):
                blk = gc // 16
                col = (gc % 16) * 128
                row0 = blk * 3072
                dst = (ggu if which == "g" else ugu)(buf)
                sem = ("gg" if which == "g" else "gu") + str(buf)
                return E.dma("sync",
                             dst.rearrange("p (dc j) -> p dc j", j=128),
                             qc_ap[row0:row0 + D, col:col + 128]
                             .rearrange("(dc p) j -> p dc j", p=128), sem)

            def proj_rows(hc):
                rank = hc // 8
                local = hc * 128 - rank * 1024
                r0 = rank * 3072 + D + local
                return qc_ap[r0:r0 + 128, :]

            E.wait("sync", "cc1", tok_ag1)
            E.wait("tensor", "ve", tok_x1)
            out_toks = []
            for half in range(2):
                hb = half * 512
                t0 = half * 4
                te_now = E.tok("te")
                E.wait("vector", "te", te_now)
                E.wait("scalar", "te", te_now)
                E.wait("sync", "te", te_now)
                E.wait("sync", "ve", tok_x1)
                x1fm_tok = [0] * 16
                for dc in range(16):
                    for tt in range(4):
                        x1fm_tok[dc] = E.dma(
                            "sync", x1fm(dc)[:, tt * 128:(tt + 1) * 128],
                            AB[:, X1T + (t0 + tt) * D + dc * 128:
                                  X1T + (t0 + tt) * D + dc * 128 + 128],
                            "xf", transpose=True)
                tok_x1fm = E.tok("xf")
                E.wait("tensor", "xf", tok_x1fm)

                gtok = [0, 0]
                utok = [0, 0]
                gtok[0] = load_gu(0, 0, "g")
                utok[0] = load_gu(64, 0, "u")
                gtok[1] = load_gu(1, 1, "g")
                utok[1] = load_gu(65, 1, "u")
                wpct = [0] * 8
                gu_free = [0, 0]
                pj_free = [0, 0, 0]
                prev_grp_pj = 0
                h2_first = True
                for g in range(8):
                    for i in range(8):
                        hc = g * 8 + i
                        E.wait("sync", "te", prev_grp_pj)
                        wpct[i] = E.dma("sync", wpc(i), proj_rows(hc), "wp")
                    for i in range(8):
                        c = g * 8 + i
                        buf = c % 2
                        pr = c % 2
                        E.wait("tensor", "gg%d" % buf, gtok[buf])
                        E.wait("tensor", "gu%d" % buf, utok[buf])
                        E.wait("tensor", "ve", gu_free[pr])
                        for dc in range(16):
                            E.op("tensor", lambda te, pr=pr, buf=buf, dc=dc: te.matmul(
                                psA[:, pr * 1024:pr * 1024 + 512],
                                ggu(buf)[:, dc * 128:(dc + 1) * 128], x1fm(dc),
                                start=(dc == 0), stop=(dc == 15)), "te")
                        for dc in range(16):
                            E.op("tensor", lambda te, pr=pr, buf=buf, dc=dc: te.matmul(
                                psA[:, pr * 1024 + 512:pr * 1024 + 1024],
                                ugu(buf)[:, dc * 128:(dc + 1) * 128], x1fm(dc),
                                start=(dc == 0), stop=(dc == 15)), "te")
                        gu_mm = E.tok("te")
                        E.wait("scalar", "te", gu_mm)
                        E.op("scalar", lambda s, pr=pr: s.activation(
                            sg(pr), psA[:, pr * 1024:pr * 1024 + 512], ACTF.Silu), "se")
                        sg_tok = E.tok("se")
                        E.wait("vector", "se", sg_tok)
                        E.op("vector", lambda v, pr=pr, i=i: v.tensor_tensor(
                            hgrp(i), sg(pr), psA[:, pr * 1024 + 512:pr * 1024 + 1024],
                            ALU.mult), "ve")
                        gu_free[pr] = E.tok("ve")
                        h_tok = gu_free[pr]
                        E.wait("scalar", "ve", h_tok)
                        E.op("scalar", lambda s, i=i: s.activation(
                            h2sb, hgrp(i), ACTF.Square), "se")
                        h2_tok = E.tok("se")
                        E.wait("tensor", "se", h2_tok)
                        E.op("tensor", lambda te, first=h2_first, last=(c == 63): te.matmul(
                            psB[0:1, 0:512], ones_col, h2sb,
                            start=first, stop=last), "te")
                        h2_first = False
                        if c + 2 < 64:
                            E.wait("sync", "te", gu_mm)
                            gtok[buf] = load_gu(c + 2, buf, "g")
                            utok[buf] = load_gu(64 + c + 2, buf, "u")
                        if debug and half == 0 and c == 0:
                            E.wait("sync", "ve", h_tok)
                            E.dma("sync", dbg["h0"], hgrp(0), "od")
                    grp_h = E.tok("ve")
                    E.wait("tensor", "ve", grp_h)
                    E.wait("tensor", "wp", E.tok("wp"))
                    for tt in range(4):
                        for sl in range(4):
                            b = (tt * 4 + sl) % 3
                            E.wait("tensor", "ve", pj_free[b])
                            for i in range(8):
                                E.op("tensor", lambda te, b=b, i=i, tt=tt, sl=sl: te.matmul(
                                    psB[:, 512 + b * 512:512 + (b + 1) * 512],
                                    hgrp(i)[:, tt * 128:(tt + 1) * 128],
                                    wpc(i)[:, sl * 512:(sl + 1) * 512],
                                    start=(i == 0), stop=(i == 7)), "te")
                            pj_mm = E.tok("te")
                            E.wait("vector", "te", pj_mm)
                            if g == 0:
                                pj_free[b] = E.op("vector", lambda v, b=b, tt=tt, sl=sl: v.tensor_copy(
                                    acc_t(tt)[:, sl * 512:(sl + 1) * 512],
                                    psB[:, 512 + b * 512:512 + (b + 1) * 512]), "ve")
                            else:
                                pj_free[b] = E.op(
                                    "vector", lambda v, b=b, tt=tt, sl=sl: v.tensor_tensor(
                                        acc_t(tt)[:, sl * 512:(sl + 1) * 512],
                                        acc_t(tt)[:, sl * 512:(sl + 1) * 512],
                                        psB[:, 512 + b * 512:512 + (b + 1) * 512], ALU.add), "ve")
                    prev_grp_pj = E.tok("te")
                tok_acc = E.tok("ve")
                if debug and half == 0:
                    E.wait("sync", "ve", tok_acc)
                    E.dma("sync", dbg["acc0"], acc_t(0), "od")

                h2done = E.tok("te")
                E.wait("vector", "te", h2done)
                E.op("vector", lambda v: v.tensor_copy(h2row[0:1, :], psB[0:1, 0:512]), "ve")
                E.wait("vector", "ve", E.tok("ve"))
                E.op("vector", lambda v: v.tensor_scalar(
                    rqr_row[0:1, :], h2row[0:1, :], 1.0 / H_FFN, EPS_RMS, ALU.mult, ALU.add), "ve")
                t_h2 = E.tok("ve")
                E.wait("scalar", "ve", t_h2)
                E.op("scalar", lambda s: s.activation(
                    h2row[0:1, :], rqr_row[0:1, :], ACTF.Sqrt), "se")
                t_sq = E.tok("se")
                E.wait("vector", "se", t_sq)
                E.op("vector", lambda v: v.reciprocal(
                    rqr_row[0:1, :], h2row[0:1, :]), "ve")
                t_rq2 = E.tok("ve")
                E.wait("tensor", "ve", tok_acc)
                E.wait("tensor", "ve", t_rq2)
                for tt in range(4):
                    mt = E.op("tensor", lambda te, tt=tt: te.transpose(
                        psA[:, tt * 512:tt * 512 + 1],
                        rqr_row[0:1, tt * 128:(tt + 1) * 128], ident_s[0:1, 0:1]), "te")
                    E.wait("vector", "te", mt)
                    E.op("vector", lambda v, tt=tt, t0=t0: v.tensor_copy(
                        rqr_tm[:, t0 + tt:t0 + tt + 1], psA[:, tt * 512:tt * 512 + 1]), "ve")
                t_rqt = E.tok("ve")
                if debug and half == 0:
                    E.wait("sync", "ve", t_rq2)
                    E.dma("sync", dbg["rqr"][0:1, 0:512], rqr_row[0:1, :], "od")

                for tt in range(4):
                    ti = t0 + tt
                    E.wait("vector", "ve", t_rqt)
                    E.op("vector", lambda v, tt=tt, ti=ti: v.scalar_tensor_tensor(
                        AF[:, Y2:Y2 + D], acc_t(tt), rqr_tm[:, ti:ti + 1],
                        AB[:, X1T + ti * D: X1T + (ti + 1) * D], ALU.mult, ALU.add), "ve")
                    ytok = E.tok("ve")
                    E.op("vector", lambda v: v.reduce_sum(
                        stats[:, 0:1], AF[:, Y2:Y2 + D], axis=AX), "ve")
                    E.wait("scalar", "ve", ytok)
                    E.op("scalar", lambda s: s.activation(
                        AF[:, ELU:ELU + 1024], AF[:, Y2:Y2 + 1024], ACTF.Square,
                        accum_out=stats[:, 1:2]), "se")
                    E.op("scalar", lambda s: s.activation(
                        AF[:, ELU:ELU + 1024], AF[:, Y2 + 1024:Y2 + 2048], ACTF.Square,
                        accum_out=stats[:, 2:3]), "se")
                    sqtok = E.tok("se")
                    E.wait("vector", "se", sqtok)
                    E.op("vector", lambda v: v.tensor_scalar(
                        stats[:, 3:4], stats[:, 0:1], 1.0 / D, None, ALU.mult), "ve")
                    E.op("vector", lambda v: v.tensor_add(
                        stats[:, 4:5], stats[:, 1:2], stats[:, 2:3]), "ve")
                    E.op("vector", lambda v: v.tensor_mul(
                        stats[:, 5:6], stats[:, 3:4], stats[:, 3:4]), "ve")
                    E.op("vector", lambda v: v.tensor_scalar(
                        stats[:, 9:10], stats[:, 4:5], 1.0 / D, None, ALU.mult), "ve")
                    E.wait("vector", "ve", E.tok("ve"))
                    E.op("vector", lambda v: v.tensor_sub(
                        stats[:, 6:7], stats[:, 9:10], stats[:, 5:6]), "ve")
                    E.wait("vector", "ve", E.tok("ve"))
                    E.op("vector", lambda v: v.tensor_scalar(
                        stats[:, 6:7], stats[:, 6:7], EPS_LN, None, ALU.add), "ve")
                    vtok = E.tok("ve")
                    E.wait("scalar", "ve", vtok)
                    E.op("scalar", lambda s: s.activation(
                        stats[:, 7:8], stats[:, 6:7], ACTF.Sqrt), "se")
                    sdtok = E.tok("se")
                    E.wait("vector", "se", sdtok)
                    E.op("vector", lambda v: v.reciprocal(stats[:, 8:9], stats[:, 7:8]), "ve")
                    E.wait("vector", "ve", E.tok("ve"))
                    E.op("vector", lambda v: v.tensor_scalar(
                        AF[:, ELU:ELU + D], AF[:, Y2:Y2 + D],
                        stats[:, 3:4], stats[:, 8:9], ALU.subtract, ALU.mult), "ve")
                    otok = E.tok("ve")
                    E.wait("sync", "ve", otok)
                    out_toks.append(E.dma(
                        "sync", y_h.rearrange("(t p) d -> t p d", p=128)[ti],
                        AF[:, ELU:ELU + D], "od"))
                    E.wait("vector", "od", out_toks[-1])

            E.wait("sync", "od", E.tok("od"))
        except _Stop:
            pass


        block = stk.enter_context(nc.Block())

        def run(eng_name):
            def body(e):
                for fn in E.q[eng_name]:
                    fn(e)
            return body

        block.tensor(run("tensor"))
        block.vector(run("vector"))
        block.scalar(run("scalar"))
        block.sync(run("sync"))
        block.gpsimd(run("gpsimd"))

    return nc, dbg


_NC_CACHE = {}


def _get_nc(debug=False, stage=99):
    key = (bool(debug), stage)
    if key not in _NC_CACHE:
        _NC_CACHE[key] = build_nc(debug, stage)
    return _NC_CACHE[key]


def make_in_maps(inputs):
    f32 = np.float32
    x = np.asarray(inputs["L0_attn_x"], f32)
    wq = np.asarray(inputs["L0_attn_wq"], f32)
    bq = np.asarray(inputs["L0_attn_bq"], f32)
    wk = np.asarray(inputs["L0_attn_wk"], f32)
    bk = np.asarray(inputs["L0_attn_bk"], f32)
    wv = np.asarray(inputs["L0_attn_wv"], f32)
    bv = np.asarray(inputs["L0_attn_bv"], f32)
    wo = np.asarray(inputs["L0_attn_wo"], f32)
    bo = np.asarray(inputs["L0_attn_bo"], f32)
    wgu = np.asarray(inputs["L0_ffn_w_gate_up"], f32)
    wpj = np.asarray(inputs["L0_ffn_w_proj"], f32)

    xf = np.ascontiguousarray(x.reshape(2 * 4096, D))
    consts = np.zeros((128, 4), dtype=ml_dtypes.bfloat16)
    consts[:, 0] = 1.0
    consts[:64, 1] = 1.0
    consts[64:, 2] = 1.0
    sel2 = np.zeros((2, 128), dtype=ml_dtypes.bfloat16)
    sel2[0, :64] = 1.0
    sel2[1, 64:] = 1.0
    hsel = np.zeros((16, 16 * 128), dtype=f32)
    for h in range(16):
        hsel[h, h * 128:(h + 1) * 128] = 1.0
    shared = {
        "wqT": np.ascontiguousarray(wq.T), "wkT": np.ascontiguousarray(wk.T),
        "wvT": np.ascontiguousarray(wv.T), "woT": np.ascontiguousarray(wo.T),
        "bq": np.ascontiguousarray(bq.reshape(1, D)),
        "bk": np.ascontiguousarray(bk.reshape(1, D)),
        "bv": np.ascontiguousarray(bv.reshape(1, D)),
        "bo": np.ascontiguousarray(bo.reshape(1, D)),
        "ident": np.eye(128, dtype=f32),
        "identb": np.eye(128, dtype=ml_dtypes.bfloat16),
        "rowone": np.ones((1, 128), dtype=ml_dtypes.bfloat16),
        "consts_bf": consts, "sel2": sel2, "hsel": hsel,
    }
    in_maps = []
    for c in range(N_CORES):
        m = dict(shared)
        m["x"] = np.ascontiguousarray(xf[c * TOK:(c + 1) * TOK])
        m["wguT"] = np.ascontiguousarray(wgu[c * D:(c + 1) * D, :].T)
        m["wprojT"] = np.ascontiguousarray(wpj[:, c * 1024:(c + 1) * 1024].T)
        bm = np.zeros((128, 2), dtype=f32)
        bm[:, c // 4] = 1.0
        m["bmask"] = bm
        in_maps.append(m)
    return in_maps


def kernel(**inputs):
    nc, _dbg = _get_nc(debug=False)
    in_maps = make_in_maps(inputs)
    res = run_bass_kernel_spmd(nc, in_maps, core_ids=list(range(N_CORES)))
    out = np.concatenate([res.results[c]["y"] for c in range(N_CORES)], axis=0)
    return out.reshape(2, 4096, D).astype(np.float32)
